# version 32
# baseline (speedup 1.0000x reference)
import numpy as np
from contextlib import ExitStack
import concourse.bass as bass
import concourse.mybir as mybir
from concourse import bass_utils as _bass_utils
from concourse.bass_utils import run_bass_kernel_spmd

if not getattr(_bass_utils, "_ldw_patch", False):
    _orig_run_command = _bass_utils.run_command

    def _run_command_ldw(argv, **kwargs):
        try:
            argv = ["--enable-ldw-opt=true" if a == "--enable-ldw-opt=false"
                    else a for a in argv]
        except TypeError:
            pass
        return _orig_run_command(argv, **kwargs)

    _bass_utils.run_command = _run_command_ldw
    _bass_utils._ldw_patch = True

F32 = mybir.dt.float32
F32R = mybir.dt.float32r
BF16 = mybir.dt.bfloat16
U8 = mybir.dt.uint8


def _r(ap):
    return ap.bitcast(F32R)
AF = mybir.ActivationFunctionType
ALU = mybir.AluOpType
AX = mybir.AxisListType

P = 128
D = 1024
H = 16
HD = 64
HFF = 4096
EPS = 1e-5
SLOT = 65

COMPUTE_ENGINES = ("pe", "act", "dve", "pool", "sp")


class Buf:
    __slots__ = ("name", "last_w", "readers")

    def __init__(self, name):
        self.name = name
        self.last_w = None
        self.readers = []


class OpRec:
    __slots__ = ("engine", "fn", "wait_ops", "extra_waits", "needs_inc",
                 "milestone", "qsem", "idx", "tag", "acc")

    def __init__(self, engine, fn, qsem=None, tag=""):
        self.acc = False
        self.engine = engine
        self.fn = fn
        self.wait_ops = []
        self.extra_waits = []
        self.needs_inc = False
        self.milestone = None
        self.qsem = qsem
        self.idx = 0
        self.tag = tag


class Sched:
    NSEM = 8

    def __init__(self, same_engine_edges=COMPUTE_ENGINES):
        self.ops = {e: [] for e in COMPUTE_ENGINES}
        self.queue_idx = {}
        self.qsem_total = {}
        self.waited_idx = {e: {} for e in COMPUTE_ENGINES}
        self.waited_q = {e: {} for e in COMPUTE_ENGINES}
        self.same_engine_edges = set(same_engine_edges)
        self.n_ops = 0

    def _dep_wait(self, rec, dep):
        eng = rec.engine
        if dep.qsem is not None:
            wm = self.waited_q[eng]
            if wm.get(dep.qsem, 0) < dep.milestone:
                wm[dep.qsem] = dep.milestone
                rec.extra_waits.append((dep.qsem, dep.milestone))
            return
        if (dep.engine == eng and rec.qsem is None
                and (eng not in self.same_engine_edges
                     or (rec.acc and dep.acc and eng == "pe"))):
            return
        wm = self.waited_idx[eng]
        if wm.get(dep.engine, -1) >= dep.idx:
            return
        wm[dep.engine] = dep.idx
        dep.needs_inc = True
        rec.wait_ops.append(dep)

    def op(self, engine, fn, reads=(), writes=(), queue=None, tag="",
           acc=False, dep_ops=()):
        qsem = None
        if queue is not None:
            i = self.queue_idx.get(queue, 0)
            self.queue_idx[queue] = i + 1
            qsem = f"{queue}{i % self.NSEM}"
        rec = OpRec(engine, fn, qsem=qsem, tag=tag)
        rec.acc = acc
        rec.idx = len(self.ops[engine])
        deps = list(dep_ops)
        for b in reads:
            if b.last_w is not None:
                deps.append(b.last_w)
        for b in writes:
            if b.last_w is not None and not b.readers:
                deps.append(b.last_w)
            deps.extend(b.readers)
        seen = set()
        for d in deps:
            if d is rec or id(d) in seen:
                continue
            seen.add(id(d))
            self._dep_wait(rec, d)
        if qsem is not None:
            prev = self.qsem_total.get(qsem, 0)
            if prev:
                wm = self.waited_q[engine]
                if wm.get(qsem, 0) < prev:
                    wm[qsem] = prev
                    rec.extra_waits.append((qsem, prev))
            rec.milestone = prev + 16
            self.qsem_total[qsem] = rec.milestone
        self.ops[engine].append(rec)
        self.n_ops += 1
        for b in writes:
            b.last_w = rec
            b.readers = []
        for b in reads:
            b.readers.append(rec)
        return rec

    def barrier(self, engines=COMPUTE_ENGINES, queues=()):
        targets = [self.ops[e][-1] for e in engines if self.ops[e]]
        qtargets = []
        for q in queues:
            for i in range(self.NSEM):
                tot = self.qsem_total.get(f"{q}{i}", 0)
                if tot:
                    qtargets.append((f"{q}{i}", tot))
        for e in engines:
            rec = OpRec(e, None, tag="barrier")
            rec.idx = len(self.ops[e])
            for t in targets:
                if t.engine == e and t.qsem is None:
                    continue
                self._dep_wait(rec, t)
            wm = self.waited_q[e]
            for sname, val in qtargets:
                if wm.get(sname, 0) < val:
                    wm[sname] = val
                    rec.extra_waits.append((sname, val))
            if rec.wait_ops or rec.extra_waits:
                self.ops[e].append(rec)

    def final_wait(self, engine, queue):
        rec = OpRec(engine, None, tag="final")
        rec.idx = len(self.ops[engine])
        wm = self.waited_q[engine]
        for i in range(self.NSEM):
            sname = f"{queue}{i}"
            tot = self.qsem_total.get(sname, 0)
            if tot and wm.get(sname, 0) < tot:
                wm[sname] = tot
                rec.extra_waits.append((sname, tot))
        if rec.extra_waits:
            self.ops[engine].append(rec)

    def finalize(self):
        for e in COMPUTE_ENGINES:
            count = 0
            for rec in self.ops[e]:
                if rec.qsem is None and rec.needs_inc:
                    count += 1
                    rec.milestone = count

    def replay(self, nc):
        self.finalize()
        semnames = set(self.qsem_total.keys())
        for e in COMPUTE_ENGINES:
            for rec in self.ops[e]:
                if rec.qsem is None and rec.needs_inc:
                    semnames.add(rec.engine)
        with ExitStack() as stack:
            sems = {s: stack.enter_context(nc.semaphore(s))
                    for s in sorted(semnames)}
            block = stack.enter_context(nc.Block())

            def runner(recs):
                def run(eng):
                    for rec in recs:
                        for dep in rec.wait_ops:
                            eng.wait_ge(sems[dep.engine], dep.milestone)
                        for sem, val in rec.extra_waits:
                            eng.wait_ge(sems[sem], val)
                        if rec.fn is None:
                            if rec.needs_inc and rec.qsem is None:
                                eng.nop().then_inc(sems[rec.engine], 1)
                            continue
                        inst = rec.fn(eng)
                        if rec.qsem is not None:
                            inst.then_inc(sems[rec.qsem], 16)
                        elif rec.needs_inc:
                            inst.then_inc(sems[rec.engine], 1)
                return run

            engmap = {"pe": block.tensor, "act": block.scalar,
                      "dve": block.vector, "pool": block.gpsimd,
                      "sp": block.sync}
            for e in COMPUTE_ENGINES:
                if self.ops[e]:
                    engmap[e](runner(self.ops[e]))


def build_program(S=4096, phases="ABC"):
    NW = S // 512
    KCN = S // 128
    nc = bass.Bass("TRN2", target_bir_lowering=False)
    s = Sched()

    x_in = nc.dram_tensor("x_in", [KCN, P, D], F32, kind="ExternalInput").ap()
    w1_in = nc.dram_tensor("w1_in", [32, P, D], F32, kind="ExternalInput").ap()
    b1_in = nc.dram_tensor("b1_in", [32, P, 1], F32, kind="ExternalInput").ap()
    w2_in = nc.dram_tensor("w2_in", [8, P, HFF], F32, kind="ExternalInput").ap()
    b2_in = nc.dram_tensor("b2_in", [1, D], F32, kind="ExternalInput").ap()
    out_t = nc.dram_tensor("out", [KCN, P, D], F32, kind="ExternalOutput").ap()
    scr_sw1t = nc.dram_tensor("scr_sw1t", [32, P, 8, P], BF16, kind="Internal").ap()
    scr_sw2t = nc.dram_tensor("scr_sw2t", [8, 2, 16, P, P], BF16,
                              kind="Internal").ap()
    scr_o = nc.dram_tensor("scr_o", [NW, H, SLOT, 512], F32, kind="Internal").ap()
    scr_a2 = nc.dram_tensor("scr_a2", [8, P, 1], F32, kind="Internal").ap()

    def sball(name, shape, dtype):
        return nc.alloc_sbuf_tensor(name, shape, dtype).ap()

    consts = sball("consts", [P, 8], F32)
    consts16 = sball("consts16", [16, 1], F32)
    ident_bf = sball("ident_bf", [P, P], BF16)
    ident_f = sball("ident_f", [P, P], F32)
    tril01 = sball("tril01", [P, P], BF16)
    sel = sball("sel", [16, 8 * P], F32)
    sel_r = sball("sel_r", [16, 8 * P], F32R)
    ones128r = sball("ones128r", [P, 1], F32R)
    ones128 = sball("ones128", [P, 1], F32)
    ones1x128 = sball("ones1x128", [1, P], F32)
    alpha1_sb = sball("alpha1_sb", [P, 32], F32)
    b1_sb = sball("b1_sb", [P, 32], F32)
    alpha2c = sball("alpha2c", [P, 8], F32)
    alpha2b = sball("alpha2b", [P, D], F32)
    b2b = sball("b2b", [P, D], F32)
    a2row = sball("a2row", [1, D], F32)
    b2row = sball("b2row", [1, D], F32)

    ps_big = nc.alloc_psum_tensor("ps_big", [P, 2048], F32).ap()
    ps_a = nc.alloc_psum_tensor("ps_a", [P, 512], F32).ap()
    ps_b = nc.alloc_psum_tensor("ps_b", [P, 512], F32).ap()
    ps_tbf = nc.alloc_psum_tensor("ps_tbf", [P, 1024], BF16).ap()
    ps_tbf2 = nc.alloc_psum_tensor("ps_tbf2", [P, 1024], BF16).ap()

    arena_h = nc.alloc_sbuf_tensor("arena", [P, nc.sbuf_bytes_remaining - 256], U8)
    base = nc.lookup_mloc(arena_h).addr
    _uid = [0]

    def at(shape, dtype, off):
        _uid[0] += 1
        h = nc.alloc_sbuf_tensor_at(f"ar{_uid[0]}", shape, dtype,
                                    offset=base + off)
        return h.ap()

    def width(shape, dtype):
        n = 1
        for d_ in shape[1:]:
            n *= d_
        return n * mybir.dt.size(dtype)

    xn_tm = at([P, KCN, H * SLOT], BF16, 0)
    o_xn = width([P, KCN, H * SLOT], BF16)
    M_bf = at([16, S], BF16, o_xn)
    AS = o_xn + width([16, S], BF16) + 64
    AS = (AS + 63) & ~63

    oAX = AS
    xraw = [at([P, D], F32, oAX + i * 4096) for i in range(4)]
    sq_sA = [at([P, D], F32, oAX + 16384 + i * 4096) for i in range(4)]
    normq = at([P, KCN, 16], F32, oAX + 32768)
    lncolA = [at([P, 16], F32, oAX + 34816 + i * 64) for i in range(4)]
    bnA = [at([P, 12], F32, oAX + 35072 + i * 64) for i in range(4)]
    bufK = [at([SLOT, S], BF16, AS + i * (S * 2)) for i in range(2)]
    oBK = 2 * S * 2
    bufQ = [at([SLOT, S], BF16, AS + oBK + i * (S * 2)) for i in range(2)]
    oBQ = oBK + 2 * S * 2
    PT = [at([P, 1024], BF16, AS + oBQ + i * 2048) for i in range(4)]
    ostg = [at([SLOT, 512], F32, AS + oBQ + 8192 + i * 2048) for i in range(2)]
    oBW = oBQ + 12288
    w1raw = [at([P, D], F32, AS + oBW + i * 4096) for i in range(4)]
    sw1sb = [at([P, D], BF16, AS + oBW + 16384 + i * 2048) for i in range(4)]
    stg1 = [at([P, 8, P], BF16, AS + oBW + 24576 + i * 2048) for i in range(4)]
    oW2 = oBW + 32768
    w2raw_h = [at([P, 2048], F32, AS + oW2 + i * 8192) for i in range(2)]
    sw2sb_h = [at([P, 2048], BF16, AS + oW2 + 16384 + i * 4096) for i in range(2)]
    stg2dh = [at([P, 16, P], BF16, AS + oW2 + 24576 + i * 4096) for i in range(2)]
    a2part = at([P, 2], F32, AS + oW2 + 32768)

    c = 0
    O_fm = at([P, 8, 512], F32, c); c += width([P, 8, 512], F32)
    denom = at([16, 512], F32, c); c += 2048
    x_win = [at([P, 4, D], F32, c + i * 16384) for i in range(2)]; c += 32768
    xn2_tm = at([P, 4, D], BF16, c); c += width([P, 4, D], BF16)
    xn2_fm = [at([P, 8, 512], BF16, c + i * 8192) for i in range(2)]; c += 16384
    g_ring = at([P, 12, 512], BF16, c); c += width([P, 12, 512], BF16)
    sw1t_sb = [at([P, 4, 8, P], BF16, c + i * 8192) for i in range(2)]; c += 16384
    sw2t_sb = [at([P, 8, D], BF16, c + i * 16384) for i in range(2)]; c += 32768
    sq_sC = [at([P, 512], F32R, c + i * 2048) for i in range(2)]; c += 4096
    o2_sb = at([P, 4, D], F32, c); c += width([P, 4, D], F32)
    outstage = [at([P, D], F32, c + i * 4096) for i in range(2)]; c += 8192
    rowt = [at([1, 512], F32, c + i * 2048) for i in range(6)]; c += 12288
    lncolC = [at([P, 16], F32, c + i * 64) for i in range(2)]; c += 128
    lncolD = [at([P, 16], F32, c + i * 64) for i in range(2)]; c += 128
    abcol = at([P, 8], F32, c); c += 32
    ystg = [at([P, 512], F32, c + i * 2048) for i in range(2)]; c += 4096
    bnst = [at([P, 12], F32, c + i * 64) for i in range(2)]; c += 128
    recS = at([16, 512], F32R, c); c += 2048
    murow, e2row, arow, tmprow, brow, varrow = [r[0:1, :] for r in rowt]
    nbrow = brow
    B_ = Buf
    consts_B = B_("consts"); consts16_B = B_("c16")
    identbf_B = B_("identbf"); identf_B = B_("identf"); tril_B = B_("tril")
    sel_B = B_("sel"); ones_B = B_("ones"); ones1_B = B_("ones1")
    selr_B = B_("selr"); onesr_B = B_("onesr"); recS_B = B_("recS")
    a1_B = B_("a1"); b1s_B = B_("b1s"); a2c_B = B_("a2c")
    a2b_B = B_("a2b"); b2b_B = B_("b2b"); a2r_B = B_("a2r"); b2r_B = B_("b2r")
    scra2_B = B_("scra2")
    psA_B = B_("psA"); psB_B = B_("psB"); tbf_B = B_("tbf"); tbf2_B = B_("tbf2")
    xn_B = B_("xn"); M_B = B_("M")
    w1raw_B = [B_(f"w1r{i}") for i in range(4)]
    sw1sb_B = [B_(f"sw1s{i}") for i in range(4)]
    stg1_B = [B_(f"stg1{i}") for i in range(4)]
    scr1_B = [B_(f"scr1_{hc}") for hc in range(32)]
    w2rawh_B = [B_(f"w2rh{i}") for i in range(2)]
    sw2sbh_B = [B_(f"sw2sh{i}") for i in range(2)]
    stg2dh_B = [B_(f"stg2dh{i}") for i in range(2)]
    a2p_B = B_("a2p")
    scr2h_B = [[B_(f"scr2_{dc}_{h}") for h in range(2)] for dc in range(8)]
    xraw_B = [B_(f"xr{i}") for i in range(4)]
    sqA_B = [B_(f"sqA{i}") for i in range(4)]
    normq_B = B_("normq")
    lncA_B = [B_(f"lncA{i}") for i in range(4)]
    bnA_B = [B_(f"bnA{i}") for i in range(4)]

    TBF = [ps_tbf, ps_tbf2]
    TBF_B = [tbf_B, tbf2_B]
    T16 = [ps_tbf[:, P * k:P * (k + 1)] for k in range(8)] + \
          [ps_tbf2[:, P * k:P * (k + 1)] for k in range(8)]
    T16_B = [B_(f"t16_{k}") for k in range(16)]
    PSAB = [ps_a, ps_b]
    PSAB_B = [psA_B, psB_B]

    for col, val in [(0, EPS), (1, 1.0 / D), (2, -1.0 / D)]:
        s.op("dve", lambda e, col=col, val=val: e.memset(consts[:, col:col + 1], val),
             writes=[consts_B])
    s.op("dve", lambda e: e.memset(consts16, 16.0), writes=[consts16_B])
    s.op("dve", lambda e: e.memset(ones128, 1.0), writes=[ones_B])
    s.op("dve", lambda e: e.memset(ones1x128, 1.0), writes=[ones1_B])
    s.op("pool", lambda e: e.memset(ident_bf, 0.0), writes=[identbf_B])
    s.op("pool", lambda e: e.affine_select(
        out=ident_bf, in_=ident_bf, compare_op=ALU.not_equal, fill=1.0,
        base=0, pattern=[[-1, P]], channel_multiplier=1),
        reads=[identbf_B], writes=[identbf_B])
    s.op("pool", lambda e: e.memset(ident_f, 0.0), writes=[identf_B])
    s.op("pool", lambda e: e.affine_select(
        out=ident_f, in_=ident_f, compare_op=ALU.not_equal, fill=1.0,
        base=0, pattern=[[-1, P]], channel_multiplier=1),
        reads=[identf_B], writes=[identf_B])
    s.op("pool", lambda e: e.memset(tril01, 1.0), writes=[tril_B])
    s.op("pool", lambda e: e.affine_select(
        out=tril01, in_=tril01, compare_op=ALU.is_ge, fill=0.0,
        base=0, pattern=[[1, P]], channel_multiplier=-1),
        reads=[tril_B], writes=[tril_B])
    s.op("pool", lambda e: e.memset(sel, 1.0), writes=[sel_B])
    s.op("pool", lambda e: e.affine_select(
        out=sel, in_=sel, compare_op=ALU.is_ge, fill=0.0,
        base=0, pattern=[[1, 8 * P]], channel_multiplier=-64),
        reads=[sel_B], writes=[sel_B])
    s.op("pool", lambda e: e.affine_select(
        out=sel, in_=sel, compare_op=ALU.is_ge, fill=0.0,
        base=63, pattern=[[-1, 8 * P]], channel_multiplier=64),
        reads=[sel_B], writes=[sel_B])
    def _selr(e):
        with nc.allow_low_precision(reason="fp32r PE operand rounding"):
            e.tensor_copy(sel_r, sel)
            return e.tensor_copy(ones128r, ones128)
    s.op("dve", _selr, reads=[sel_B, ones_B], writes=[selr_B, onesr_B])

    def x_stage1(tc):
        par = tc % 4
        lc, lcB = lncolA[par], lncA_B[par]
        bs, bsB = bnA[par], bnA_B[par]
        s.op("sp", lambda e, tc=tc, par=par: e.dma_start(
            out=xraw[par], in_=x_in[tc]), writes=[xraw_B[par]], queue="spq")
        xr, xrB = xraw[par], xraw_B[par]
        for hf in range(2):
            s.op("dve", lambda e, xr=xr, bs=bs, hf=hf: e.bn_stats(
                bs[:, 6 * hf:6 * hf + 6], xr[:, 512 * hf:512 * (hf + 1)]),
                reads=[xrB], writes=[bsB])
        s.op("dve", lambda e, lc=lc, bs=bs: e.bn_aggr(lc[:, 0:2], bs),
             reads=[bsB], writes=[lcB])
        s.op("dve", lambda e, lc=lc: e.tensor_scalar_mul(
            lc[:, 4:5], lc[:, 0:1], -1.0), reads=[lcB], writes=[lcB])
        s.op("act", lambda e, lc=lc: e.activation(
            lc[:, 2:3], lc[:, 1:2], AF.Sqrt, bias=consts[:, 0:1]),
            reads=[lcB, consts_B], writes=[lcB])

    def x_stage2(tc):
        par = tc % 4
        lc, lcB = lncolA[par], lncA_B[par]
        xr, xrB = xraw[par], xraw_B[par]
        xr3 = xr.rearrange("p (h c) -> p h c", c=HD)
        s.op("dve", lambda e, lc=lc: e.reciprocal(lc[:, 3:4], lc[:, 2:3]),
             reads=[lcB], writes=[lcB])
        xn3 = xn_tm[:, tc, :].rearrange("p (h c) -> p h c", c=SLOT)
        s.op("dve", lambda e, xr3=xr3, xn3=xn3, lc=lc: e.tensor_scalar(
            out=xn3[:, :, 0:HD], in0=xr3, scalar1=lc[:, 4:5],
            scalar2=lc[:, 3:4], op0=ALU.add, op1=ALU.mult),
            reads=[xrB, lcB], writes=[xn_B])
        s.op("pool", lambda e, xn3=xn3: e.memset(xn3[:, :, HD:SLOT], 1.0),
             writes=[xn_B])
        sq3 = sq_sA[par].rearrange("p (h c) -> p h c", c=HD)
        s.op("pool", lambda e, xn3=xn3, sq3=sq3: e.tensor_mul(
            sq3, xn3[:, :, 0:HD], xn3[:, :, 0:HD]),
            reads=[xn_B], writes=[sqA_B[par]])

    def x_stage3(tc):
        par = tc % 4
        tp = tc % 2
        sq3 = sq_sA[par].rearrange("p (h c) -> p h c", c=HD)
        s.op("dve", lambda e, tc=tc, sq3=sq3, par=par: e.reduce_sum(
            normq[:, tc, :], sq3, axis=AX.X),
            reads=[sqA_B[par]], writes=[normq_B])
        s.op("pe", lambda e, tc=tc, tp=tp: e.transpose(
            PSAB[tp][0:16, 0:P], normq[:, tc, :], ident_f),
            reads=[normq_B, identf_B], writes=[PSAB_B[tp]])
        s.op("act", lambda e, tc=tc, tp=tp: e.activation(
            M_bf[:, P * tc:P * (tc + 1)], PSAB[tp][0:16, 0:P], AF.Sqrt,
            scale=consts16),
            reads=[PSAB_B[tp], consts16_B], writes=[M_B])

    def w2_dma(dc, h):
        s.op("sp", lambda e, dc=dc, h=h: e.dma_start(
            out=w2raw_h[h], in_=w2_in[dc][:, 2048 * h:2048 * (h + 1)]),
            writes=[w2rawh_B[h]], queue="spq")

    def w2_unit_steps(dc):
        steps = []

        def s0(dc=dc):
            w2_dma(dc, 0)

        def s1(dc=dc):
            w2_dma(dc, 1)
            s.op("act", lambda e: e.activation(
                sw2sb_h[0], w2raw_h[0], AF.Sign),
                reads=[w2rawh_B[0]], writes=[sw2sbh_B[0]])
            s.op("dve", lambda e: e.reduce_sum(
                a2part[:, 0:1], w2raw_h[0], axis=AX.X,
                apply_absolute_value=True),
                reads=[w2rawh_B[0]], writes=[a2p_B])

        def s2(dc=dc):
            s.op("act", lambda e: e.activation(
                sw2sb_h[1], w2raw_h[1], AF.Sign),
                reads=[w2rawh_B[1]], writes=[sw2sbh_B[1]])
            s.op("dve", lambda e: e.reduce_sum(
                a2part[:, 1:2], w2raw_h[1], axis=AX.X,
                apply_absolute_value=True),
                reads=[w2rawh_B[1]], writes=[a2p_B])
            s.op("dve", lambda e, dc=dc: e.tensor_add(
                alpha2c[:, dc:dc + 1], a2part[:, 0:1], a2part[:, 1:2]),
                reads=[a2p_B], writes=[a2c_B])
            s.op("dve", lambda e, dc=dc: e.tensor_scalar_mul(
                alpha2c[:, dc:dc + 1], alpha2c[:, dc:dc + 1], 1.0 / HFF),
                reads=[a2c_B], writes=[a2c_B])
            for h in range(2):
                s.op("dve", lambda e, dc=dc, h=h: e.tensor_scalar_mul(
                    sw2sb_h[h], sw2sb_h[h], alpha2c[:, dc:dc + 1]),
                    reads=[sw2sbh_B[h], a2c_B], writes=[sw2sbh_B[h]])
        steps += [s0, s1, s2]
        for h in range(2):
            for g0 in (0, 8):
                def stw(dc=dc, h=h, g0=g0):
                    tb = (g0 // 8) % 2
                    for i in range(g0, g0 + 8):
                        s.op("pe", lambda e, h=h, i=i, g0=g0, tb=tb:
                             e.transpose(
                                 TBF[tb][:, P * (i - g0):P * (i - g0 + 1)],
                                 sw2sb_h[h][:, P * i:P * (i + 1)], ident_bf),
                             reads=[sw2sbh_B[h], identbf_B],
                             writes=[TBF_B[tb]])
                    s.op("dve", lambda e, h=h, g0=g0, tb=tb: e.tensor_copy(
                        stg2dh[h][:, g0:g0 + 8, :].rearrange(
                            "p c q -> p (c q)"), TBF[tb]),
                        reads=[TBF_B[tb]], writes=[stg2dh_B[h]])
                steps.append(stw)

                def stc(dc=dc, h=h, g0=g0):
                    if g0 == 8:
                        s.op("pool", lambda e, dc=dc, h=h: e.dma_start(
                            out=scr_sw2t[dc, h].rearrange("h p q -> p h q"),
                            in_=stg2dh[h]),
                            reads=[stg2dh_B[h]], writes=[scr2h_B[dc][h]],
                            queue="poolq")
                steps.append(stc)
        return steps

    def w1_dma(hc):
        par = hc % 4
        s.op("sp", lambda e, hc=hc, par=par: e.dma_start(
            out=w1raw[par], in_=w1_in[hc]), writes=[w1raw_B[par]], queue="spq")
        s.op("sp", lambda e, hc=hc: e.dma_start(
            out=b1_sb[:, hc:hc + 1], in_=b1_in[hc]), writes=[b1s_B],
            queue="spq")

    def w1_unit_steps(hc):
        par = hc % 4
        steps = []

        def s0(hc=hc, par=par):
            if hc + 1 < 32:
                w1_dma(hc + 1)
            s.op("act", lambda e, par=par: e.activation(
                sw1sb[par], w1raw[par], AF.Sign),
                reads=[w1raw_B[par]], writes=[sw1sb_B[par]])
            s.op("dve", lambda e, hc=hc, par=par: e.reduce_sum(
                alpha1_sb[:, hc:hc + 1], w1raw[par], axis=AX.X,
                apply_absolute_value=True),
                reads=[w1raw_B[par]], writes=[a1_B])
            s.op("dve", lambda e, hc=hc: e.tensor_scalar_mul(
                alpha1_sb[:, hc:hc + 1], alpha1_sb[:, hc:hc + 1], 1.0 / D),
                reads=[a1_B], writes=[a1_B])
        steps.append(s0)
        def st1(hc=hc, par=par):
            tb = hc % 2
            for c_ in range(8):
                s.op("pe", lambda e, par=par, c_=c_, tb=tb: e.transpose(
                    TBF[tb][:, P * c_:P * (c_ + 1)],
                    sw1sb[par][:, P * c_:P * (c_ + 1)], ident_bf),
                    reads=[sw1sb_B[par], identbf_B], writes=[TBF_B[tb]])
            s.op("dve", lambda e, par=par, tb=tb: e.tensor_copy(
                stg1[par].rearrange("p c q -> p (c q)"), TBF[tb]),
                reads=[TBF_B[tb]], writes=[stg1_B[par]])

        def st2(hc=hc, par=par):
            s.op("pool", lambda e, hc=hc, par=par: e.dma_start(
                out=scr_sw1t[hc], in_=stg1[par]),
                reads=[stg1_B[par]], writes=[scr1_B[hc]],
                queue="poolq")
        steps.append(st1)
        steps.append(st2)
        return steps

    for tc in range(KCN + 4):
        if tc < KCN:
            x_stage1(tc)
        if 2 <= tc < KCN + 2:
            x_stage2(tc - 2)
        if tc >= 4:
            x_stage3(tc - 4)
    s.op("sp", lambda e: e.dma_start(out=b2row, in_=b2_in),
         writes=[b2r_B], queue="spq")
    for half in range(2):
        sl = slice(512 * half, 512 * (half + 1))
        s.op("pe", lambda e, sl=sl: e.matmul(
            ps_b, ones1x128, b2row[:, sl], start=True, stop=True),
            reads=[ones1_B, b2r_B], writes=[psB_B])
        s.op("dve", lambda e, sl=sl: e.tensor_copy(b2b[:, sl], ps_b),
             reads=[psB_B], writes=[b2b_B])
    s.barrier(queues=("spq", "poolq"))
    if phases == "A":
        s.final_wait("pool", "poolq")
        s.final_wait("sp", "spq")
        s.replay(nc)
        return nc

    bufK_B = [B_(f"bufK{i}") for i in range(2)]
    bufQ_B = [B_(f"bufQ{i}") for i in range(2)]
    PT_B = [B_(f"PT{i}") for i in range(4)]
    ostg_B = [B_(f"ostg{i}") for i in range(2)]
    psg = [ps_big[:, 0:1024], ps_big[:, 1024:2048]]
    psg_B = [[B_("psg00"), B_("psg01")], [B_("psg10"), B_("psg11")]]
    pso_B = [psA_B, psB_B]
    scro_B = [[B_(f"scro{w}_{h}") for h in range(H)] for w in range(NW)]

    def build_head_steps(h):
        hb = h % 2
        steps = []

        def s0(hb=hb):
            s.op("pool", lambda e, hb=hb: e.memset(bufK[hb][64:SLOT, :], -1.0),
                 writes=[bufK_B[hb]])
        steps.append(s0)
        for kc0 in range(0, KCN, 8):
            def sk(h=h, hb=hb, kc0=kc0):
                tb = (kc0 // 8) % 2
                ke = min(kc0 + 8, KCN)
                for kc in range(kc0, ke):
                    s.op("pe", lambda e, h=h, kc=kc, kc0=kc0, tb=tb:
                         e.transpose(
                             TBF[tb][0:HD, P * (kc - kc0):P * (kc - kc0 + 1)],
                             xn_tm[:, kc, SLOT * h:SLOT * h + HD], ident_bf),
                         reads=[xn_B, identbf_B], writes=[TBF_B[tb]])
                s.op("dve", lambda e, hb=hb, kc0=kc0, ke=ke, tb=tb:
                     e.tensor_copy(
                         bufK[hb][0:HD, P * kc0:P * ke],
                         TBF[tb][0:HD, 0:P * (ke - kc0)]),
                     reads=[TBF_B[tb]], writes=[bufK_B[hb]])
            steps.append(sk)

        def s1(hb=hb, h=h):
            s.op("dve", lambda e, hb=hb: e.tensor_scalar_mul(
                bufQ[hb][0:HD, :], bufK[hb][0:HD, :], 0.125),
                reads=[bufK_B[hb]], writes=[bufQ_B[hb]])
            s.op("pool", lambda e, hb=hb, h=h: e.dma_start(
                out=bufQ[hb][HD:SLOT, :], in_=M_bf[h:h + 1, :]),
                reads=[M_B], writes=[bufQ_B[hb]], queue="poolq")
        steps.append(s1)
        return steps

    def build_head(h):
        for st_ in build_head_steps(h):
            st_()

    def emit_avs(p, absorb=None):
        h, w, op_, cols, gp_, last = (p["h"], p["w"], p["op"], p["cols"],
                                      p["gp"], p["last"])
        nk = 4 * (w + 1)
        pso = PSAB[op_]
        for ci, (kc, j, j0, c0) in enumerate(cols):
            dep = [absorb] if (absorb is not None and ci == len(cols) - 1) \
                else []
            s.op("pe", lambda e, h=h, kc=kc, j=j, c0=c0, gp=gp_, pso=pso,
                 st=(kc == 0), sp_=(kc == nk - 1):
                 e.matmul(pso[0:SLOT, c0:512],
                          xn_tm[:, kc, SLOT * h:SLOT * (h + 1)],
                          PT[gp][:, 512 * j + c0:512 * (j + 1)],
                          start=st, stop=sp_),
                 reads=[xn_B, PT_B[gp_]], writes=[pso_B[op_]], acc=True,
                 dep_ops=dep)
        if last:
            s.op("dve", lambda e, op_=op_: e.tensor_copy(
                ostg[op_], PSAB[op_][0:SLOT, :]),
                reads=[pso_B[op_]], writes=[ostg_B[op_]])
            s.op("pool", lambda e, w=w, h=h, op_=op_: e.dma_start(
                out=scr_o[w, h], in_=ostg[op_]),
                reads=[ostg_B[op_]], writes=[scro_B[w][h]], queue="poolq")

    build_head(0)
    pendq = []
    gpc = 0
    exp_prev = exp_last = None
    headq = []
    w1_dma(0)
    units = []
    _w2u = [w2_unit_steps(dc) for dc in range(8)]
    for hc in range(32):
        units.append(w1_unit_steps(hc))
        if hc % 4 == 3 and _w2u:
            units.append(_w2u.pop(0))
    units.extend(_w2u)
    w1sq = []
    active = []
    ui = 0
    while True:
        while len(active) < 3 and ui < len(units):
            active.append(iter(units[ui]))
            ui += 1
        if not active:
            break
        it = active.pop(0)
        try:
            w1sq.append(next(it))
            active.append(it)
        except StopIteration:
            pass
    for h in range(H):
        hb = h % 2
        for w in range(NW):
            op_ = (h * NW + w) % 2
            nk = 4 * (w + 1)
            for g in range(0, nk, 2):
                sp2 = gpc % 2
                gp = gpc % 4
                gpc += 1
                cols = []
                for j in range(2):
                    kc = g + j
                    j0 = kc - 4 * w
                    c0 = 128 * j0 if j0 >= 0 else 0
                    cols.append((kc, j, j0, c0))
                    s.op("pe", lambda e, hb=hb, kc=kc, j=j, c0=c0, sp2=sp2,
                         w=w:
                         e.matmul(psg[sp2][:, 512 * j + c0:512 * (j + 1)],
                                  bufK[hb][0:SLOT, P * kc:P * (kc + 1)],
                                  bufQ[hb][0:SLOT, 512 * w + c0:512 * (w + 1)],
                                  start=True, stop=True),
                         reads=[bufK_B[hb], bufQ_B[hb]], writes=[psg_B[sp2][j]])
                if cols[0][2] >= 0:
                    for (kc, j, j0, c0) in cols:
                        er = s.op("act", lambda e, j=j, c0=c0, gp=gp, sp2=sp2:
                                  e.activation(
                                      PT[gp][:, 512 * j + c0:512 * (j + 1)],
                                      psg[sp2][:, 512 * j + c0:512 * (j + 1)],
                                      AF.Exp),
                                  reads=[psg_B[sp2][j]], writes=[PT_B[gp]])
                else:
                    er = s.op("act", lambda e, gp=gp, sp2=sp2: e.activation(
                        PT[gp], psg[sp2], AF.Exp),
                        reads=[psg_B[sp2][0], psg_B[sp2][1]], writes=[PT_B[gp]])
                exp_prev, exp_last = exp_last, er
                for (kc, j, j0, c0) in cols:
                    if j0 >= 0:
                        s.op("dve", lambda e, j=j, c0=c0, gp=gp: e.tensor_mul(
                            PT[gp][:, 512 * j + c0:512 * j + c0 + P],
                            PT[gp][:, 512 * j + c0:512 * j + c0 + P], tril01),
                            reads=[PT_B[gp], tril_B], writes=[PT_B[gp]])
                pendq.append(dict(h=h, w=w, op=op_, cols=cols, gp=gp,
                                  last=(g + 2 >= nk)))
                if len(pendq) > 2:
                    emit_avs(pendq.pop(0))
                if g in (2, 4, 6):
                    if headq:
                        headq.pop(0)()
                    elif w1sq:
                        w1sq.pop(0)()
            if w == 1 and h + 1 < H:
                headq.extend(build_head_steps(h + 1))
            if w == NW - 1:
                while headq:
                    headq.pop(0)()
            if NW == 1 and h + 1 < H:
                build_head(h + 1)
    while w1sq:
        w1sq.pop(0)()
    for p in pendq:
        emit_avs(p)
    s.barrier(queues=("spq", "poolq"))
    if phases == "AB":
        s.final_wait("pool", "poolq")
        s.final_wait("sp", "spq")
        s.replay(nc)
        return nc

    Ofm_B = B_("Ofm"); den_B = B_("den")
    xwin_B = [B_("xwin0"), B_("xwin1")]
    xn2tm_B = B_("xn2tm"); xn2fm_B = [B_("xn2fm0"), B_("xn2fm1")]
    gring_B = [B_(f"g{i}") for i in range(12)]
    sw1t_B = [B_(f"sw1t{i}") for i in range(3)]
    sw2t_B = [B_(f"sw2t{i}") for i in range(2)]
    gtmp_B = [B_(f"gtmp{i}") for i in range(2)]
    sqC_B = [B_(f"sqC{i}") for i in range(2)]
    o2_B = [B_(f"o2_{t}") for t in range(4)]
    xmC_B = [B_(f"xmC{i}") for i in range(2)]
    outst_B = [B_(f"outst{i}") for i in range(2)]
    rows_B = B_("rows")
    abc_B = B_("abc")
    ystg_B = [B_(f"ystg{i}") for i in range(2)]
    bnst_B = [B_(f"bnst{i}") for i in range(2)]
    lncC_B = [B_(f"lncC{i}") for i in range(2)]
    lncD_B = [B_(f"lncD{i}") for i in range(2)]
    cps = [ps_big[:, 512 * i:512 * (i + 1)] for i in range(4)]
    cps_B = [B_(f"cps{i}") for i in range(4)]
    cps2 = ps_big[:, 1024:2048]
    cps2_B = B_("cps2w")

    def c_loads_od(w):
        s.op("pool", lambda e, w=w: e.dma_start(
            out=denom, in_=scr_o[w, :, 64, :]),
            reads=[scro_B[w][hh] for hh in range(H)], writes=[den_B],
            queue="poolq")
        for r in range(2):
            s.op("pool", lambda e, w=w, r=r: e.dma_start(
                out=O_fm[64 * r:64 * r + 64, :, :],
                in_=scr_o[w, r::2, 0:64, :].rearrange("g f q -> f g q")),
                reads=[scro_B[w][hh] for hh in range(r, H, 2)], writes=[Ofm_B],
                queue="poolq")

    def c_loads_x(w):
        wp = w % 2
        s.op("pool", lambda e, w=w, wp=wp: e.dma_start(
            out=x_win[wp],
            in_=x_in[4 * w:4 * w + 4].rearrange("t p d -> p t d")),
            writes=[xwin_B[wp]], queue="poolq")

    def c_loads(w):
        c_loads_od(w)
        c_loads_x(w)

    def c_stageA1(w):
        def _recs(e):
            with nc.allow_low_precision(reason="fp32r PE operand rounding"):
                return e.reciprocal(recS, denom)
        s.op("dve", _recs, reads=[den_B], writes=[recS_B])
        for c_ in range(8):
            pb, pbB = (ps_a, psA_B) if c_ % 2 == 0 else (ps_b, psB_B)
            s.op("pe", lambda e, c_=c_, pb=pb: e.matmul(
                pb, sel_r[:, P * c_:P * (c_ + 1)], recS,
                start=True, stop=True),
                reads=[selr_B, recS_B], writes=[pbB])
            s.op("dve", lambda e, c_=c_, pb=pb: e.tensor_mul(
                O_fm[:, c_, :], O_fm[:, c_, :], pb),
                reads=[Ofm_B, pbB], writes=[Ofm_B])

    def c_stageA2(w):
        for c_ in range(8):
            s.op("pe", lambda e, c_=c_: e.matmul(
                ps_a[0:1, :], ones128, O_fm[:, c_, :],
                start=(c_ == 0), stop=(c_ == 7)),
                reads=[ones_B, Ofm_B], writes=[psA_B], acc=True)
        s.op("dve", lambda e: e.tensor_scalar_mul(murow, ps_a[0:1, :], 1.0 / D),
             reads=[psA_B], writes=[rows_B])
        for c_ in range(8):
            sp_ = c_ % 2
            def _sq(e, c_=c_, sp_=sp_):
                with nc.allow_low_precision(
                        reason="fp32r PE operand rounding"):
                    return e.tensor_mul(
                        sq_sC[sp_], O_fm[:, c_, :], O_fm[:, c_, :])
            s.op("dve", _sq, reads=[Ofm_B], writes=[sqC_B[sp_]])
            s.op("pe", lambda e, c_=c_, sp_=sp_: e.matmul(
                ps_b[0:1, :], ones128r, sq_sC[sp_],
                start=(c_ == 0), stop=(c_ == 7)),
                reads=[onesr_B, sqC_B[sp_]], writes=[psB_B], acc=True)
        s.op("dve", lambda e: e.tensor_scalar_mul(e2row, ps_b[0:1, :], 1.0 / D),
             reads=[psB_B], writes=[rows_B])
        s.op("dve", lambda e: e.tensor_mul(tmprow, murow, murow),
             reads=[rows_B], writes=[rows_B])
        s.op("dve", lambda e: e.tensor_sub(varrow, e2row, tmprow),
             reads=[rows_B], writes=[rows_B])

    def c_stageB1(w):
        wp = w % 2
        s.op("act", lambda e: e.activation(tmprow, varrow, AF.Sqrt,
                                           bias=consts[0:1, 0:1]),
             reads=[rows_B, consts_B], writes=[rows_B])
        s.op("dve", lambda e: e.reciprocal(arow, tmprow),
             reads=[rows_B], writes=[rows_B])
        s.op("dve", lambda e: e.scalar_tensor_tensor(
            out=nbrow, in0=murow, scalar=-1.0, in1=arow,
            op0=ALU.mult, op1=ALU.mult),
            reads=[rows_B], writes=[rows_B])
        for t in range(4):
            tp = t % 2
            s.op("pe", lambda e, t=t, tp=tp: e.transpose(
                PSAB[tp][:, 0:1], arow[:, P * t:P * (t + 1)],
                ident_f[0:1, 0:1]),
                reads=[rows_B, identf_B], writes=[PSAB_B[tp]])
            s.op("pe", lambda e, t=t, tp=tp: e.transpose(
                PSAB[tp][:, 1:2], nbrow[:, P * t:P * (t + 1)],
                ident_f[0:1, 0:1]),
                reads=[rows_B, identf_B], writes=[PSAB_B[tp]])
            s.op("dve", lambda e, t=t, tp=tp: e.tensor_copy(
                abcol[:, 2 * t:2 * t + 2], PSAB[tp][:, 0:2]),
                reads=[PSAB_B[tp]], writes=[abc_B])
        for t in range(4):
            for cb in range(2):
                tp = (t * 2 + cb) % 2
                yp = (t * 2 + cb) % 2
                for c_ in range(4 * cb, 4 * cb + 4):
                    s.op("pe", lambda e, c_=c_, t=t, tp=tp, cb=cb: e.transpose(
                        PSAB[tp][:, P * (c_ - 4 * cb):P * (c_ - 4 * cb + 1)],
                        O_fm[:, c_, P * t:P * (t + 1)], ident_f),
                        reads=[Ofm_B, identf_B], writes=[PSAB_B[tp]],
                        acc=True)
                s.op("dve", lambda e, t=t, tp=tp, yp=yp: e.tensor_scalar(
                    out=ystg[yp], in0=PSAB[tp],
                    scalar1=abcol[:, 2 * t:2 * t + 1],
                    scalar2=abcol[:, 2 * t + 1:2 * t + 2],
                    op0=ALU.mult, op1=ALU.add),
                    reads=[PSAB_B[tp], abc_B], writes=[ystg_B[yp]])
                s.op("pool", lambda e, cb=cb, t=t, yp=yp, wp=wp: e.tensor_add(
                    x_win[wp][:, t, 512 * cb:512 * (cb + 1)],
                    x_win[wp][:, t, 512 * cb:512 * (cb + 1)], ystg[yp]),
                    reads=[ystg_B[yp], xwin_B[wp]], writes=[xwin_B[wp]])
        for t in range(4):
            par, b = t % 2, 8 * (t // 2)
            lc, lcB = lncolC[par], lncC_B[par]
            bs, bsB = bnst[par], bnst_B[par]
            for hf in range(2):
                s.op("dve", lambda e, t=t, bs=bs, hf=hf, wp=wp: e.bn_stats(
                    bs[:, 6 * hf:6 * hf + 6],
                    x_win[wp][:, t, 512 * hf:512 * (hf + 1)]),
                    reads=[xwin_B[wp]], writes=[bsB])
            s.op("dve", lambda e, lc=lc, b=b, bs=bs: e.bn_aggr(
                lc[:, b + 2:b + 4], bs),
                reads=[bsB], writes=[lcB])
        for t in range(4):
            par, b = t % 2, 8 * (t // 2)
            lc, lcB = lncolC[par], lncC_B[par]
            s.op("act", lambda e, lc=lc, b=b: e.activation(
                lc[:, b + 5:b + 6], lc[:, b + 3:b + 4], AF.Sqrt,
                bias=consts[:, 0:1]),
                reads=[lcB, consts_B], writes=[lcB])
        for t in range(4):
            par, b = t % 2, 8 * (t // 2)
            lc, lcB = lncolC[par], lncC_B[par]
            s.op("dve", lambda e, lc=lc, b=b: e.reciprocal(
                lc[:, b + 6:b + 7], lc[:, b + 5:b + 6]),
                reads=[lcB], writes=[lcB])
            s.op("dve", lambda e, lc=lc, b=b: e.scalar_tensor_tensor(
                out=lc[:, b + 7:b + 8], in0=lc[:, b + 2:b + 3],
                scalar=-1.0, in1=lc[:, b + 6:b + 7],
                op0=ALU.mult, op1=ALU.mult),
                reads=[lcB], writes=[lcB])
        for t in range(4):
            par, b = t % 2, 8 * (t // 2)
            lc, lcB = lncolC[par], lncC_B[par]
            s.op("dve", lambda e, t=t, lc=lc, b=b, wp=wp: e.tensor_scalar(
                out=xn2_tm[:, t, :], in0=x_win[wp][:, t, :],
                scalar1=lc[:, b + 6:b + 7], scalar2=lc[:, b + 7:b + 8],
                op0=ALU.mult, op1=ALU.add),
                reads=[xwin_B[wp], lcB], writes=[xn2tm_B])
    def c_stageB2(w):
        wp = w % 2
        for c0 in range(0, 8, 2):
            tb = (c0 // 2) % 2
            for c_ in (c0, c0 + 1):
                for t in range(4):
                    sl = (c_ - c0) * 4 + t
                    s.op("pe", lambda e, c_=c_, t=t, sl=sl, tb=tb: e.transpose(
                        TBF[tb][:, P * sl:P * (sl + 1)],
                        xn2_tm[:, t, P * c_:P * (c_ + 1)], ident_bf),
                        reads=[xn2tm_B, identbf_B], writes=[TBF_B[tb]])
            s.op("dve", lambda e, c0=c0, tb=tb, wp=wp: e.tensor_copy(
                xn2_fm[wp][:, c0:c0 + 2, :].rearrange("p c q -> p (c q)"),
                TBF[tb]),
                reads=[TBF_B[tb]], writes=[xn2fm_B[wp]])

    def c_epi_part1(w, t):
        par, b = t % 2, 8 * (t // 2)
        lc, lcB = lncolD[par], lncD_B[par]
        s.op("dve", lambda e, t=t: e.tensor_add(
            o2_sb[:, t, :], o2_sb[:, t, :], b2b),
            reads=[o2_B[t], b2b_B], writes=[o2_B[t]])
        bs, bsB = bnst[par], bnst_B[par]
        for hf in range(2):
            s.op("dve", lambda e, t=t, bs=bs, hf=hf: e.bn_stats(
                bs[:, 6 * hf:6 * hf + 6],
                o2_sb[:, t, 512 * hf:512 * (hf + 1)]),
                reads=[o2_B[t]], writes=[bsB])
        s.op("dve", lambda e, lc=lc, b=b, bs=bs: e.bn_aggr(
            lc[:, b + 2:b + 4], bs),
            reads=[bsB], writes=[lcB])

    def c_epi_tail(w):
        wp = w % 2
        for t in range(4):
            par, b = t % 2, 8 * (t // 2)
            lc, lcB = lncolD[par], lncD_B[par]
            s.op("act", lambda e, lc=lc, b=b: e.activation(
                lc[:, b + 5:b + 6], lc[:, b + 3:b + 4], AF.Sqrt,
                bias=consts[:, 0:1]),
                reads=[lcB, consts_B], writes=[lcB])
        for t in range(4):
            par, b = t % 2, 8 * (t // 2)
            lc, lcB = lncolD[par], lncD_B[par]
            s.op("dve", lambda e, lc=lc, b=b: e.reciprocal(
                lc[:, b + 6:b + 7], lc[:, b + 5:b + 6]),
                reads=[lcB], writes=[lcB])
            s.op("dve", lambda e, lc=lc, b=b: e.scalar_tensor_tensor(
                out=lc[:, b + 7:b + 8], in0=lc[:, b + 2:b + 3],
                scalar=-1.0, in1=lc[:, b + 6:b + 7],
                op0=ALU.mult, op1=ALU.mult),
                reads=[lcB], writes=[lcB])
        for t in range(4):
            par, b = t % 2, 8 * (t // 2)
            lc, lcB = lncolD[par], lncD_B[par]
            s.op("dve", lambda e, t=t, par=par, lc=lc, b=b: e.tensor_scalar(
                out=o2_sb[:, t, :], in0=o2_sb[:, t, :],
                scalar1=lc[:, b + 6:b + 7], scalar2=lc[:, b + 7:b + 8],
                op0=ALU.mult, op1=ALU.add),
                reads=[o2_B[t], lcB], writes=[o2_B[t]])
            s.op("dve", lambda e, t=t, par=par, wp=wp: e.tensor_add(
                outstage[par], x_win[wp][:, t, :], o2_sb[:, t, :]),
                reads=[xwin_B[wp], o2_B[t]], writes=[outst_B[par]])
            s.op("pool", lambda e, w=w, t=t, par=par: e.dma_start(
                out=out_t[4 * w + t], in_=outstage[par]),
                reads=[outst_B[par]], queue="poolq")

    c_loads(0)
    c_stageA1(0)
    c_stageA2(0)
    c_stageB1(0)
    c_stageB2(0)
    if NW > 1:
        c_loads_od(1)
        c_stageA1(1)
    sw1_next = [0]

    def emit_sw1_load():
        gi = sw1_next[0]
        if gi >= 8 * NW:
            return
        sw1_next[0] += 1
        q = gi % 8
        s.op("sp", lambda e, q=q, gi=gi: e.dma_start(
            out=sw1t_sb[gi % 2],
            in_=scr_sw1t[4 * q:4 * q + 4].rearrange("h p c q -> p h c q")),
            reads=[scr1_B[hc] for hc in range(4 * q, 4 * q + 4)],
            writes=[sw1t_B[gi % 2]], queue="spq")

    emit_sw1_load()
    emit_sw1_load()
    for w in range(NW):
        wp = w % 2
        for grp in range(4):
            gb = grp % 2
            for k in range(8):
                hc = 8 * grp + k
                s.op("sp", lambda e, hc=hc, k=k, gb=gb: e.dma_start(
                    out=sw2t_sb[gb][:, k, :].rearrange(
                        "p (d q) -> p d q", q=P),
                    in_=scr_sw2t[:, hc // 16, hc % 16].rearrange(
                        "d p q -> p d q")),
                    reads=[scr2h_B[dc][hc // 16] for dc in range(8)],
                    writes=[sw2t_B[gb]], queue="spq")
            for k in range(8):
                hc = 8 * grp + k
                par = (hc // 4) % 2
                slot = hc % 12
                pq = hc % 2
                for c_ in range(8):
                    s.op("pe", lambda e, hc=hc, par=par, c_=c_, pq=pq, wp=wp:
                         e.matmul(
                             cps[pq], sw1t_sb[par][:, hc % 4, c_, :],
                             xn2_fm[wp][:, c_, :],
                             start=(c_ == 0), stop=(c_ == 7)),
                         reads=[sw1t_B[par], xn2fm_B[wp]], writes=[cps_B[pq]],
                         acc=True)
                if k % 4 == 3:
                    emit_sw1_load()
                s.op("act", lambda e, hc=hc, slot=slot, pq=pq: e.activation(
                    g_ring[:, slot, :], cps[pq], AF.Gelu,
                    bias=b1_sb[:, hc:hc + 1], scale=alpha1_sb[:, hc:hc + 1]),
                    reads=[cps_B[pq], b1s_B, a1_B], writes=[gring_B[slot]])
            for t in range(4):
                for dh in range(2):
                    f2 = 2 + (t * 2 + dh) % 2
                    for k in range(8):
                        hc = 8 * grp + k
                        slot = hc % 12
                        s.op("pe", lambda e, slot=slot, t=t, dh=dh, k=k, gb=gb,
                             f2=f2: e.matmul(
                                 cps[f2], g_ring[:, slot, P * t:P * (t + 1)],
                                 sw2t_sb[gb][:, k, 512 * dh:512 * (dh + 1)],
                                 start=(k == 0), stop=(k == 7)),
                             reads=[gring_B[slot], sw2t_B[gb]],
                             writes=[cps_B[f2]], acc=True)
                    if grp == 0:
                        s.op("dve", lambda e, t=t, dh=dh, f2=f2: e.tensor_copy(
                            o2_sb[:, t, 512 * dh:512 * (dh + 1)], cps[f2]),
                            reads=[cps_B[f2]], writes=[o2_B[t]])
                    else:
                        s.op("dve", lambda e, t=t, dh=dh, f2=f2: e.tensor_add(
                            o2_sb[:, t, 512 * dh:512 * (dh + 1)],
                            o2_sb[:, t, 512 * dh:512 * (dh + 1)], cps[f2]),
                            reads=[cps_B[f2], o2_B[t]], writes=[o2_B[t]])
                    if grp == 3 and dh == 1:
                        c_epi_part1(w, t)
            if grp == 0 and w + 1 < NW:
                c_stageA2(w + 1)
            elif grp == 1 and w + 1 < NW:
                c_loads_x(w + 1)
                c_stageB1(w + 1)
            elif grp == 2:
                if w + 1 < NW:
                    c_stageB2(w + 1)
                if w + 2 < NW:
                    c_loads_od(w + 2)
            elif grp == 3 and w + 2 < NW:
                c_stageA1(w + 2)
        c_epi_tail(w)
    nc._dbg = dict(O_fm=O_fm, denom=denom, x_win=x_win, xn2_tm=xn2_tm,
                   o2_sb=o2_sb, rowt=rowt, M_bf=M_bf, xn_tm=xn_tm)
    s.final_wait("pool", "poolq")
    s.final_wait("sp", "spq")
    s.replay(nc)
    return nc


def kernel(L0_attn_x, L0_ffn_w1, L0_ffn_b1, L0_ffn_w2, L0_ffn_b2):
    S = 4096
    x = np.ascontiguousarray(np.asarray(L0_attn_x, dtype=np.float32))
    w1 = np.ascontiguousarray(np.asarray(L0_ffn_w1, dtype=np.float32)).reshape(32, P, D)
    b1 = np.ascontiguousarray(np.asarray(L0_ffn_b1, dtype=np.float32)).reshape(32, P, 1)
    w2 = np.ascontiguousarray(np.asarray(L0_ffn_w2, dtype=np.float32)).reshape(8, P, HFF)
    b2 = np.ascontiguousarray(np.asarray(L0_ffn_b2, dtype=np.float32)).reshape(1, D)
    nc = build_program(S)
    in_maps = [{"x_in": np.ascontiguousarray(x[b].reshape(S // P, P, D)),
                "w1_in": w1, "b1_in": b1, "w2_in": w2, "b2_in": b2}
               for b in range(8)]
    res = run_bass_kernel_spmd(nc, in_maps, core_ids=list(range(8)))
    return np.stack([np.asarray(r["out"]).reshape(S, D) for r in res.results],
                    axis=0)


# version 33
# speedup vs baseline: 1.0461x; 1.0461x over previous
import numpy as np
from contextlib import ExitStack
import concourse.bass as bass
import concourse.mybir as mybir
from concourse import bass_utils as _bass_utils
from concourse.bass_utils import run_bass_kernel_spmd

if not getattr(_bass_utils, "_ldw_patch", False):
    _orig_run_command = _bass_utils.run_command

    def _run_command_ldw(argv, **kwargs):
        try:
            argv = ["--enable-ldw-opt=true" if a == "--enable-ldw-opt=false"
                    else a for a in argv]
        except TypeError:
            pass
        return _orig_run_command(argv, **kwargs)

    _bass_utils.run_command = _run_command_ldw
    _bass_utils._ldw_patch = True

F32 = mybir.dt.float32
F32R = mybir.dt.float32r
BF16 = mybir.dt.bfloat16
U8 = mybir.dt.uint8


def _r(ap):
    return ap.bitcast(F32R)
AF = mybir.ActivationFunctionType
ALU = mybir.AluOpType
AX = mybir.AxisListType

P = 128
D = 1024
H = 16
HD = 64
HFF = 4096
EPS = 1e-5
SLOT = 65

COMPUTE_ENGINES = ("pe", "act", "dve", "pool", "sp")


class Buf:
    __slots__ = ("name", "last_w", "readers")

    def __init__(self, name):
        self.name = name
        self.last_w = None
        self.readers = []


class OpRec:
    __slots__ = ("engine", "fn", "wait_ops", "extra_waits", "needs_inc",
                 "milestone", "qsem", "idx", "tag", "acc")

    def __init__(self, engine, fn, qsem=None, tag=""):
        self.acc = False
        self.engine = engine
        self.fn = fn
        self.wait_ops = []
        self.extra_waits = []
        self.needs_inc = False
        self.milestone = None
        self.qsem = qsem
        self.idx = 0
        self.tag = tag


class Sched:
    NSEM = 8

    def __init__(self, same_engine_edges=COMPUTE_ENGINES):
        self.ops = {e: [] for e in COMPUTE_ENGINES}
        self.queue_idx = {}
        self.qsem_total = {}
        self.waited_idx = {e: {} for e in COMPUTE_ENGINES}
        self.waited_q = {e: {} for e in COMPUTE_ENGINES}
        self.same_engine_edges = set(same_engine_edges)
        self.n_ops = 0

    def _dep_wait(self, rec, dep):
        eng = rec.engine
        if dep.qsem is not None:
            wm = self.waited_q[eng]
            if wm.get(dep.qsem, 0) < dep.milestone:
                wm[dep.qsem] = dep.milestone
                rec.extra_waits.append((dep.qsem, dep.milestone))
            return
        if (dep.engine == eng and rec.qsem is None
                and (eng not in self.same_engine_edges
                     or (rec.acc and dep.acc and eng == "pe"))):
            return
        wm = self.waited_idx[eng]
        if wm.get(dep.engine, -1) >= dep.idx:
            return
        wm[dep.engine] = dep.idx
        dep.needs_inc = True
        rec.wait_ops.append(dep)

    def op(self, engine, fn, reads=(), writes=(), queue=None, tag="",
           acc=False, dep_ops=()):
        qsem = None
        if queue is not None:
            i = self.queue_idx.get(queue, 0)
            self.queue_idx[queue] = i + 1
            qsem = f"{queue}{i % self.NSEM}"
        rec = OpRec(engine, fn, qsem=qsem, tag=tag)
        rec.acc = acc
        rec.idx = len(self.ops[engine])
        deps = list(dep_ops)
        for b in reads:
            if b.last_w is not None:
                deps.append(b.last_w)
        for b in writes:
            if b.last_w is not None and not b.readers:
                deps.append(b.last_w)
            deps.extend(b.readers)
        seen = set()
        for d in deps:
            if d is rec or id(d) in seen:
                continue
            seen.add(id(d))
            self._dep_wait(rec, d)
        if qsem is not None:
            prev = self.qsem_total.get(qsem, 0)
            if prev:
                wm = self.waited_q[engine]
                if wm.get(qsem, 0) < prev:
                    wm[qsem] = prev
                    rec.extra_waits.append((qsem, prev))
            rec.milestone = prev + 16
            self.qsem_total[qsem] = rec.milestone
        self.ops[engine].append(rec)
        self.n_ops += 1
        for b in writes:
            b.last_w = rec
            b.readers = []
        for b in reads:
            b.readers.append(rec)
        return rec

    def barrier(self, engines=COMPUTE_ENGINES, queues=()):
        targets = [self.ops[e][-1] for e in engines if self.ops[e]]
        qtargets = []
        for q in queues:
            for i in range(self.NSEM):
                tot = self.qsem_total.get(f"{q}{i}", 0)
                if tot:
                    qtargets.append((f"{q}{i}", tot))
        for e in engines:
            rec = OpRec(e, None, tag="barrier")
            rec.idx = len(self.ops[e])
            for t in targets:
                if t.engine == e and t.qsem is None:
                    continue
                self._dep_wait(rec, t)
            wm = self.waited_q[e]
            for sname, val in qtargets:
                if wm.get(sname, 0) < val:
                    wm[sname] = val
                    rec.extra_waits.append((sname, val))
            if rec.wait_ops or rec.extra_waits:
                self.ops[e].append(rec)

    def final_wait(self, engine, queue):
        rec = OpRec(engine, None, tag="final")
        rec.idx = len(self.ops[engine])
        wm = self.waited_q[engine]
        for i in range(self.NSEM):
            sname = f"{queue}{i}"
            tot = self.qsem_total.get(sname, 0)
            if tot and wm.get(sname, 0) < tot:
                wm[sname] = tot
                rec.extra_waits.append((sname, tot))
        if rec.extra_waits:
            self.ops[engine].append(rec)

    def finalize(self):
        for e in COMPUTE_ENGINES:
            count = 0
            for rec in self.ops[e]:
                if rec.qsem is None and rec.needs_inc:
                    count += 1
                    rec.milestone = count

    def replay(self, nc):
        self.finalize()
        semnames = set(self.qsem_total.keys())
        for e in COMPUTE_ENGINES:
            for rec in self.ops[e]:
                if rec.qsem is None and rec.needs_inc:
                    semnames.add(rec.engine)
        with ExitStack() as stack:
            sems = {s: stack.enter_context(nc.semaphore(s))
                    for s in sorted(semnames)}
            block = stack.enter_context(nc.Block())

            def runner(recs):
                def run(eng):
                    for rec in recs:
                        for dep in rec.wait_ops:
                            eng.wait_ge(sems[dep.engine], dep.milestone)
                        for sem, val in rec.extra_waits:
                            eng.wait_ge(sems[sem], val)
                        if rec.fn is None:
                            if rec.needs_inc and rec.qsem is None:
                                eng.nop().then_inc(sems[rec.engine], 1)
                            continue
                        inst = rec.fn(eng)
                        if rec.qsem is not None:
                            inst.then_inc(sems[rec.qsem], 16)
                        elif rec.needs_inc:
                            inst.then_inc(sems[rec.engine], 1)
                return run

            engmap = {"pe": block.tensor, "act": block.scalar,
                      "dve": block.vector, "pool": block.gpsimd,
                      "sp": block.sync}
            for e in COMPUTE_ENGINES:
                if self.ops[e]:
                    engmap[e](runner(self.ops[e]))


def build_program(S=4096, phases="ABC"):
    NW = S // 512
    KCN = S // 128
    nc = bass.Bass("TRN2", target_bir_lowering=False)
    s = Sched()

    x_in = nc.dram_tensor("x_in", [KCN, P, D], F32, kind="ExternalInput").ap()
    w1_in = nc.dram_tensor("w1_in", [32, P, D], F32, kind="ExternalInput").ap()
    b1_in = nc.dram_tensor("b1_in", [32, P, 1], F32, kind="ExternalInput").ap()
    w2_in = nc.dram_tensor("w2_in", [8, P, HFF], F32, kind="ExternalInput").ap()
    b2_in = nc.dram_tensor("b2_in", [1, D], F32, kind="ExternalInput").ap()
    out_t = nc.dram_tensor("out", [KCN, P, D], F32, kind="ExternalOutput").ap()
    scr_sw1t = nc.dram_tensor("scr_sw1t", [32, P, 8, P], BF16, kind="Internal").ap()
    scr_sw2t = nc.dram_tensor("scr_sw2t", [8, 2, 16, P, P], BF16,
                              kind="Internal").ap()
    scr_o = nc.dram_tensor("scr_o", [NW, H, SLOT, 512], F32, kind="Internal").ap()
    scr_a2 = nc.dram_tensor("scr_a2", [8, P, 1], F32, kind="Internal").ap()

    def sball(name, shape, dtype):
        return nc.alloc_sbuf_tensor(name, shape, dtype).ap()

    consts = sball("consts", [P, 8], F32)
    consts16 = sball("consts16", [16, 1], F32)
    ident_bf = sball("ident_bf", [P, P], BF16)
    ident_f = sball("ident_f", [P, P], F32)
    tril01 = sball("tril01", [P, P], BF16)
    sel = sball("sel", [16, 8 * P], F32)
    sel_r = sball("sel_r", [16, 8 * P], F32R)
    ones128r = sball("ones128r", [P, 1], F32R)
    ones128 = sball("ones128", [P, 1], F32)
    ones1x128 = sball("ones1x128", [1, P], F32)
    alpha1_sb = sball("alpha1_sb", [P, 32], F32)
    b1_sb = sball("b1_sb", [P, 32], F32)
    alpha2c = sball("alpha2c", [P, 8], F32)
    alpha2b = sball("alpha2b", [P, D], F32)
    b2b = sball("b2b", [P, D], F32)
    a2row = sball("a2row", [1, D], F32)
    b2row = sball("b2row", [1, D], F32)

    ps_big = nc.alloc_psum_tensor("ps_big", [P, 2048], F32).ap()
    ps_a = nc.alloc_psum_tensor("ps_a", [P, 512], F32).ap()
    ps_b = nc.alloc_psum_tensor("ps_b", [P, 512], F32).ap()
    ps_tbf = nc.alloc_psum_tensor("ps_tbf", [P, 1024], BF16).ap()
    ps_tbf2 = nc.alloc_psum_tensor("ps_tbf2", [P, 1024], BF16).ap()

    arena_h = nc.alloc_sbuf_tensor("arena", [P, nc.sbuf_bytes_remaining - 256], U8)
    base = nc.lookup_mloc(arena_h).addr
    _uid = [0]

    def at(shape, dtype, off):
        _uid[0] += 1
        h = nc.alloc_sbuf_tensor_at(f"ar{_uid[0]}", shape, dtype,
                                    offset=base + off)
        return h.ap()

    def width(shape, dtype):
        n = 1
        for d_ in shape[1:]:
            n *= d_
        return n * mybir.dt.size(dtype)

    xn_tm = at([P, KCN, H * SLOT], BF16, 0)
    o_xn = width([P, KCN, H * SLOT], BF16)
    M_bf = at([16, S], BF16, o_xn)
    AS = o_xn + width([16, S], BF16) + 64
    AS = (AS + 63) & ~63

    oAX = AS
    xraw = [at([P, D], F32, oAX + i * 4096) for i in range(4)]
    sq_sA = [at([P, D], F32, oAX + 16384 + i * 4096) for i in range(4)]
    normq = at([P, KCN, 16], F32, oAX + 32768)
    lncolA = [at([P, 16], F32, oAX + 34816 + i * 64) for i in range(4)]
    bnA = [at([P, 12], F32, oAX + 35072 + i * 64) for i in range(4)]
    bufK = [at([SLOT, S], BF16, AS + i * (S * 2)) for i in range(2)]
    oBK = 2 * S * 2
    bufQ = [at([SLOT, S], BF16, AS + oBK + i * (S * 2)) for i in range(2)]
    oBQ = oBK + 2 * S * 2
    PT = [at([P, 1024], BF16, AS + oBQ + i * 2048) for i in range(4)]
    ostg = [at([SLOT, 512], F32, AS + oBQ + 8192 + i * 2048) for i in range(2)]
    oBW = oBQ + 12288
    w1raw = [at([P, D], F32, AS + oBW + i * 4096) for i in range(4)]
    sw1sb = [at([P, D], BF16, AS + oBW + 16384 + i * 2048) for i in range(4)]
    stg1 = [at([P, 8, P], BF16, AS + oBW + 24576 + i * 2048) for i in range(4)]
    oW2 = oBW + 32768
    w2raw_h = [at([P, 2048], F32, AS + oW2 + i * 8192) for i in range(2)]
    sw2sb_h = [at([P, 2048], BF16, AS + oW2 + 16384 + i * 4096) for i in range(2)]
    stg2dh = [at([P, 16, P], BF16, AS + oW2 + 24576 + i * 4096) for i in range(2)]
    a2part = at([P, 2], F32, AS + oW2 + 32768)

    c = 0
    O_fm = at([P, 8, 512], F32, c); c += width([P, 8, 512], F32)
    denom = at([16, 512], F32, c); c += 2048
    x_win = [at([P, 4, D], F32, c + i * 16384) for i in range(2)]; c += 32768
    xn2_tm = at([P, 4, D], BF16, c); c += width([P, 4, D], BF16)
    xn2_fm = [at([P, 8, 512], BF16, c + i * 8192) for i in range(2)]; c += 16384
    g_ring = at([P, 16, 512], BF16, c); c += width([P, 16, 512], BF16)
    sw1t_sb = [at([P, 4, 8, P], BF16, c + i * 8192) for i in range(2)]; c += 16384
    sw2t_sb = [at([P, 8, D], BF16, c + i * 16384) for i in range(2)]; c += 32768
    sq_sC = [at([P, 512], F32R, c + i * 2048) for i in range(2)]; c += 4096
    o2_sb = at([P, 4, D], F32, c); c += width([P, 4, D], F32)
    outstage = [at([P, D], F32, c + i * 4096) for i in range(2)]; c += 8192
    rowt = [at([1, 512], F32, c + i * 2048) for i in range(4)]; c += 8192
    lncolC = [at([P, 16], F32, c + i * 64) for i in range(2)]; c += 128
    lncolD = [at([P, 16], F32, c + i * 64) for i in range(2)]; c += 128
    abcol = at([P, 8], F32, c); c += 32
    ystg = [at([P, 512], F32, c + i * 2048) for i in range(2)]; c += 4096
    bnst = [at([P, 12], F32, c + i * 64) for i in range(2)]; c += 128
    recS = at([16, 512], F32R, c); c += 2048
    murow, e2row, arow, tmprow = [r[0:1, :] for r in rowt]
    varrow = e2row
    nbrow = tmprow
    B_ = Buf
    consts_B = B_("consts"); consts16_B = B_("c16")
    identbf_B = B_("identbf"); identf_B = B_("identf"); tril_B = B_("tril")
    sel_B = B_("sel"); ones_B = B_("ones"); ones1_B = B_("ones1")
    selr_B = B_("selr"); onesr_B = B_("onesr"); recS_B = B_("recS")
    a1_B = B_("a1"); b1s_B = B_("b1s"); a2c_B = B_("a2c")
    a2b_B = B_("a2b"); b2b_B = B_("b2b"); a2r_B = B_("a2r"); b2r_B = B_("b2r")
    scra2_B = B_("scra2")
    psA_B = B_("psA"); psB_B = B_("psB"); tbf_B = B_("tbf"); tbf2_B = B_("tbf2")
    xn_B = B_("xn"); M_B = B_("M")
    w1raw_B = [B_(f"w1r{i}") for i in range(4)]
    sw1sb_B = [B_(f"sw1s{i}") for i in range(4)]
    stg1_B = [B_(f"stg1{i}") for i in range(4)]
    scr1_B = [B_(f"scr1_{hc}") for hc in range(32)]
    w2rawh_B = [B_(f"w2rh{i}") for i in range(2)]
    sw2sbh_B = [B_(f"sw2sh{i}") for i in range(2)]
    stg2dh_B = [B_(f"stg2dh{i}") for i in range(2)]
    a2p_B = B_("a2p")
    scr2h_B = [[B_(f"scr2_{dc}_{h}") for h in range(2)] for dc in range(8)]
    xraw_B = [B_(f"xr{i}") for i in range(4)]
    sqA_B = [B_(f"sqA{i}") for i in range(4)]
    normq_B = B_("normq")
    lncA_B = [B_(f"lncA{i}") for i in range(4)]
    bnA_B = [B_(f"bnA{i}") for i in range(4)]

    TBF = [ps_tbf, ps_tbf2]
    TBF_B = [tbf_B, tbf2_B]
    T16 = [ps_tbf[:, P * k:P * (k + 1)] for k in range(8)] + \
          [ps_tbf2[:, P * k:P * (k + 1)] for k in range(8)]
    T16_B = [B_(f"t16_{k}") for k in range(16)]
    PSAB = [ps_a, ps_b]
    PSAB_B = [psA_B, psB_B]

    for col, val in [(0, EPS), (1, 1.0 / D), (2, -1.0 / D)]:
        s.op("dve", lambda e, col=col, val=val: e.memset(consts[:, col:col + 1], val),
             writes=[consts_B])
    s.op("dve", lambda e: e.memset(consts16, 16.0), writes=[consts16_B])
    s.op("dve", lambda e: e.memset(ones128, 1.0), writes=[ones_B])
    s.op("dve", lambda e: e.memset(ones1x128, 1.0), writes=[ones1_B])
    s.op("pool", lambda e: e.memset(ident_bf, 0.0), writes=[identbf_B])
    s.op("pool", lambda e: e.affine_select(
        out=ident_bf, in_=ident_bf, compare_op=ALU.not_equal, fill=1.0,
        base=0, pattern=[[-1, P]], channel_multiplier=1),
        reads=[identbf_B], writes=[identbf_B])
    s.op("pool", lambda e: e.memset(ident_f, 0.0), writes=[identf_B])
    s.op("pool", lambda e: e.affine_select(
        out=ident_f, in_=ident_f, compare_op=ALU.not_equal, fill=1.0,
        base=0, pattern=[[-1, P]], channel_multiplier=1),
        reads=[identf_B], writes=[identf_B])
    s.op("pool", lambda e: e.memset(tril01, 1.0), writes=[tril_B])
    s.op("pool", lambda e: e.affine_select(
        out=tril01, in_=tril01, compare_op=ALU.is_ge, fill=0.0,
        base=0, pattern=[[1, P]], channel_multiplier=-1),
        reads=[tril_B], writes=[tril_B])
    s.op("pool", lambda e: e.memset(sel, 1.0), writes=[sel_B])
    s.op("pool", lambda e: e.affine_select(
        out=sel, in_=sel, compare_op=ALU.is_ge, fill=0.0,
        base=0, pattern=[[1, 8 * P]], channel_multiplier=-64),
        reads=[sel_B], writes=[sel_B])
    s.op("pool", lambda e: e.affine_select(
        out=sel, in_=sel, compare_op=ALU.is_ge, fill=0.0,
        base=63, pattern=[[-1, 8 * P]], channel_multiplier=64),
        reads=[sel_B], writes=[sel_B])
    def _selr(e):
        with nc.allow_low_precision(reason="fp32r PE operand rounding"):
            e.tensor_copy(sel_r, sel)
            return e.tensor_copy(ones128r, ones128)
    s.op("dve", _selr, reads=[sel_B, ones_B], writes=[selr_B, onesr_B])

    def x_stage1(tc):
        par = tc % 4
        lc, lcB = lncolA[par], lncA_B[par]
        bs, bsB = bnA[par], bnA_B[par]
        s.op("sp", lambda e, tc=tc, par=par: e.dma_start(
            out=xraw[par], in_=x_in[tc]), writes=[xraw_B[par]], queue="spq")
        xr, xrB = xraw[par], xraw_B[par]
        for hf in range(2):
            s.op("dve", lambda e, xr=xr, bs=bs, hf=hf: e.bn_stats(
                bs[:, 6 * hf:6 * hf + 6], xr[:, 512 * hf:512 * (hf + 1)]),
                reads=[xrB], writes=[bsB])
        s.op("dve", lambda e, lc=lc, bs=bs: e.bn_aggr(lc[:, 0:2], bs),
             reads=[bsB], writes=[lcB])
        s.op("dve", lambda e, lc=lc: e.tensor_scalar_mul(
            lc[:, 4:5], lc[:, 0:1], -1.0), reads=[lcB], writes=[lcB])
        s.op("act", lambda e, lc=lc: e.activation(
            lc[:, 2:3], lc[:, 1:2], AF.Sqrt, bias=consts[:, 0:1]),
            reads=[lcB, consts_B], writes=[lcB])

    def x_stage2(tc):
        par = tc % 4
        lc, lcB = lncolA[par], lncA_B[par]
        xr, xrB = xraw[par], xraw_B[par]
        xr3 = xr.rearrange("p (h c) -> p h c", c=HD)
        s.op("dve", lambda e, lc=lc: e.reciprocal(lc[:, 3:4], lc[:, 2:3]),
             reads=[lcB], writes=[lcB])
        xn3 = xn_tm[:, tc, :].rearrange("p (h c) -> p h c", c=SLOT)
        s.op("dve", lambda e, xr3=xr3, xn3=xn3, lc=lc: e.tensor_scalar(
            out=xn3[:, :, 0:HD], in0=xr3, scalar1=lc[:, 4:5],
            scalar2=lc[:, 3:4], op0=ALU.add, op1=ALU.mult),
            reads=[xrB, lcB], writes=[xn_B])
        s.op("pool", lambda e, xn3=xn3: e.memset(xn3[:, :, HD:SLOT], 1.0),
             writes=[xn_B])
        sq3 = sq_sA[par].rearrange("p (h c) -> p h c", c=HD)
        s.op("pool", lambda e, xn3=xn3, sq3=sq3: e.tensor_mul(
            sq3, xn3[:, :, 0:HD], xn3[:, :, 0:HD]),
            reads=[xn_B], writes=[sqA_B[par]])

    def x_stage3(tc):
        par = tc % 4
        tp = tc % 2
        sq3 = sq_sA[par].rearrange("p (h c) -> p h c", c=HD)
        s.op("dve", lambda e, tc=tc, sq3=sq3, par=par: e.reduce_sum(
            normq[:, tc, :], sq3, axis=AX.X),
            reads=[sqA_B[par]], writes=[normq_B])
        s.op("pe", lambda e, tc=tc, tp=tp: e.transpose(
            PSAB[tp][0:16, 0:P], normq[:, tc, :], ident_f),
            reads=[normq_B, identf_B], writes=[PSAB_B[tp]])
        s.op("act", lambda e, tc=tc, tp=tp: e.activation(
            M_bf[:, P * tc:P * (tc + 1)], PSAB[tp][0:16, 0:P], AF.Sqrt,
            scale=consts16),
            reads=[PSAB_B[tp], consts16_B], writes=[M_B])

    def w2_dma(dc, h):
        s.op("sp", lambda e, dc=dc, h=h: e.dma_start(
            out=w2raw_h[h], in_=w2_in[dc][:, 2048 * h:2048 * (h + 1)]),
            writes=[w2rawh_B[h]], queue="spq")

    def w2_unit_steps(dc):
        steps = []

        def s0(dc=dc):
            w2_dma(dc, 0)

        def s1(dc=dc):
            w2_dma(dc, 1)
            s.op("act", lambda e: e.activation(
                sw2sb_h[0], w2raw_h[0], AF.Sign),
                reads=[w2rawh_B[0]], writes=[sw2sbh_B[0]])
            s.op("dve", lambda e: e.reduce_sum(
                a2part[:, 0:1], w2raw_h[0], axis=AX.X,
                apply_absolute_value=True),
                reads=[w2rawh_B[0]], writes=[a2p_B])

        def s2(dc=dc):
            s.op("act", lambda e: e.activation(
                sw2sb_h[1], w2raw_h[1], AF.Sign),
                reads=[w2rawh_B[1]], writes=[sw2sbh_B[1]])
            s.op("dve", lambda e: e.reduce_sum(
                a2part[:, 1:2], w2raw_h[1], axis=AX.X,
                apply_absolute_value=True),
                reads=[w2rawh_B[1]], writes=[a2p_B])
            s.op("dve", lambda e, dc=dc: e.tensor_add(
                alpha2c[:, dc:dc + 1], a2part[:, 0:1], a2part[:, 1:2]),
                reads=[a2p_B], writes=[a2c_B])
            s.op("dve", lambda e, dc=dc: e.tensor_scalar_mul(
                alpha2c[:, dc:dc + 1], alpha2c[:, dc:dc + 1], 1.0 / HFF),
                reads=[a2c_B], writes=[a2c_B])
            for h in range(2):
                s.op("dve", lambda e, dc=dc, h=h: e.tensor_scalar_mul(
                    sw2sb_h[h], sw2sb_h[h], alpha2c[:, dc:dc + 1]),
                    reads=[sw2sbh_B[h], a2c_B], writes=[sw2sbh_B[h]])
        steps += [s0, s1, s2]
        for h in range(2):
            for g0 in (0, 8):
                def stw(dc=dc, h=h, g0=g0):
                    tb = (g0 // 8) % 2
                    for i in range(g0, g0 + 8):
                        s.op("pe", lambda e, h=h, i=i, g0=g0, tb=tb:
                             e.transpose(
                                 TBF[tb][:, P * (i - g0):P * (i - g0 + 1)],
                                 sw2sb_h[h][:, P * i:P * (i + 1)], ident_bf),
                             reads=[sw2sbh_B[h], identbf_B],
                             writes=[TBF_B[tb]])
                    s.op("dve", lambda e, h=h, g0=g0, tb=tb: e.tensor_copy(
                        stg2dh[h][:, g0:g0 + 8, :].rearrange(
                            "p c q -> p (c q)"), TBF[tb]),
                        reads=[TBF_B[tb]], writes=[stg2dh_B[h]])
                steps.append(stw)

                def stc(dc=dc, h=h, g0=g0):
                    if g0 == 8:
                        s.op("pool", lambda e, dc=dc, h=h: e.dma_start(
                            out=scr_sw2t[dc, h].rearrange("h p q -> p h q"),
                            in_=stg2dh[h]),
                            reads=[stg2dh_B[h]], writes=[scr2h_B[dc][h]],
                            queue="poolq")
                steps.append(stc)
        return steps

    def w1_dma(hc):
        par = hc % 4
        s.op("sp", lambda e, hc=hc, par=par: e.dma_start(
            out=w1raw[par], in_=w1_in[hc]), writes=[w1raw_B[par]], queue="spq")
        s.op("sp", lambda e, hc=hc: e.dma_start(
            out=b1_sb[:, hc:hc + 1], in_=b1_in[hc]), writes=[b1s_B],
            queue="spq")

    def w1_unit_steps(hc):
        par = hc % 4
        steps = []

        def s0(hc=hc, par=par):
            if hc + 1 < 32:
                w1_dma(hc + 1)
            s.op("act", lambda e, par=par: e.activation(
                sw1sb[par], w1raw[par], AF.Sign),
                reads=[w1raw_B[par]], writes=[sw1sb_B[par]])
            s.op("dve", lambda e, hc=hc, par=par: e.reduce_sum(
                alpha1_sb[:, hc:hc + 1], w1raw[par], axis=AX.X,
                apply_absolute_value=True),
                reads=[w1raw_B[par]], writes=[a1_B])
            s.op("dve", lambda e, hc=hc: e.tensor_scalar_mul(
                alpha1_sb[:, hc:hc + 1], alpha1_sb[:, hc:hc + 1], 1.0 / D),
                reads=[a1_B], writes=[a1_B])
        steps.append(s0)
        def st1(hc=hc, par=par):
            tb = hc % 2
            for c_ in range(8):
                s.op("pe", lambda e, par=par, c_=c_, tb=tb: e.transpose(
                    TBF[tb][:, P * c_:P * (c_ + 1)],
                    sw1sb[par][:, P * c_:P * (c_ + 1)], ident_bf),
                    reads=[sw1sb_B[par], identbf_B], writes=[TBF_B[tb]])
            s.op("dve", lambda e, par=par, tb=tb: e.tensor_copy(
                stg1[par].rearrange("p c q -> p (c q)"), TBF[tb]),
                reads=[TBF_B[tb]], writes=[stg1_B[par]])

        def st2(hc=hc, par=par):
            s.op("pool", lambda e, hc=hc, par=par: e.dma_start(
                out=scr_sw1t[hc], in_=stg1[par]),
                reads=[stg1_B[par]], writes=[scr1_B[hc]],
                queue="poolq")
        steps.append(st1)
        steps.append(st2)
        return steps

    for tc in range(KCN + 4):
        if tc < KCN:
            x_stage1(tc)
        if 2 <= tc < KCN + 2:
            x_stage2(tc - 2)
        if tc >= 4:
            x_stage3(tc - 4)
    s.op("sp", lambda e: e.dma_start(out=b2row, in_=b2_in),
         writes=[b2r_B], queue="spq")
    for half in range(2):
        sl = slice(512 * half, 512 * (half + 1))
        s.op("pe", lambda e, sl=sl: e.matmul(
            ps_b, ones1x128, b2row[:, sl], start=True, stop=True),
            reads=[ones1_B, b2r_B], writes=[psB_B])
        s.op("dve", lambda e, sl=sl: e.tensor_copy(b2b[:, sl], ps_b),
             reads=[psB_B], writes=[b2b_B])
    s.barrier(queues=("spq", "poolq"))
    if phases == "A":
        s.final_wait("pool", "poolq")
        s.final_wait("sp", "spq")
        s.replay(nc)
        return nc

    bufK_B = [B_(f"bufK{i}") for i in range(2)]
    bufQ_B = [B_(f"bufQ{i}") for i in range(2)]
    PT_B = [B_(f"PT{i}") for i in range(4)]
    ostg_B = [B_(f"ostg{i}") for i in range(2)]
    psg = [ps_big[:, 0:1024], ps_big[:, 1024:2048]]
    psg_B = [[B_("psg00"), B_("psg01")], [B_("psg10"), B_("psg11")]]
    pso_B = [psA_B, psB_B]
    scro_B = [[B_(f"scro{w}_{h}") for h in range(H)] for w in range(NW)]

    def build_head_steps(h):
        hb = h % 2
        steps = []

        def s0(hb=hb):
            s.op("pool", lambda e, hb=hb: e.memset(bufK[hb][64:SLOT, :], -1.0),
                 writes=[bufK_B[hb]])
        steps.append(s0)
        for kc0 in range(0, KCN, 8):
            def sk(h=h, hb=hb, kc0=kc0):
                tb = (kc0 // 8) % 2
                ke = min(kc0 + 8, KCN)
                for kc in range(kc0, ke):
                    s.op("pe", lambda e, h=h, kc=kc, kc0=kc0, tb=tb:
                         e.transpose(
                             TBF[tb][0:HD, P * (kc - kc0):P * (kc - kc0 + 1)],
                             xn_tm[:, kc, SLOT * h:SLOT * h + HD], ident_bf),
                         reads=[xn_B, identbf_B], writes=[TBF_B[tb]])
                s.op("dve", lambda e, hb=hb, kc0=kc0, ke=ke, tb=tb:
                     e.tensor_copy(
                         bufK[hb][0:HD, P * kc0:P * ke],
                         TBF[tb][0:HD, 0:P * (ke - kc0)]),
                     reads=[TBF_B[tb]], writes=[bufK_B[hb]])
            steps.append(sk)

        def s1(hb=hb, h=h):
            s.op("dve", lambda e, hb=hb: e.tensor_scalar_mul(
                bufQ[hb][0:HD, :], bufK[hb][0:HD, :], 0.125),
                reads=[bufK_B[hb]], writes=[bufQ_B[hb]])
            s.op("pool", lambda e, hb=hb, h=h: e.dma_start(
                out=bufQ[hb][HD:SLOT, :], in_=M_bf[h:h + 1, :]),
                reads=[M_B], writes=[bufQ_B[hb]], queue="poolq")
        steps.append(s1)
        return steps

    def build_head(h):
        for st_ in build_head_steps(h):
            st_()

    def emit_avs(p, absorb=None):
        h, w, op_, cols, gp_, last = (p["h"], p["w"], p["op"], p["cols"],
                                      p["gp"], p["last"])
        nk = 4 * (w + 1)
        pso = PSAB[op_]
        for ci, (kc, j, j0, c0) in enumerate(cols):
            dep = [absorb] if (absorb is not None and ci == len(cols) - 1) \
                else []
            s.op("pe", lambda e, h=h, kc=kc, j=j, c0=c0, gp=gp_, pso=pso,
                 st=(kc == 0), sp_=(kc == nk - 1):
                 e.matmul(pso[0:SLOT, c0:512],
                          xn_tm[:, kc, SLOT * h:SLOT * (h + 1)],
                          PT[gp][:, 512 * j + c0:512 * (j + 1)],
                          start=st, stop=sp_),
                 reads=[xn_B, PT_B[gp_]], writes=[pso_B[op_]], acc=True,
                 dep_ops=dep)
        if last:
            s.op("dve", lambda e, op_=op_: e.tensor_copy(
                ostg[op_], PSAB[op_][0:SLOT, :]),
                reads=[pso_B[op_]], writes=[ostg_B[op_]])
            s.op("pool", lambda e, w=w, h=h, op_=op_: e.dma_start(
                out=scr_o[w, h], in_=ostg[op_]),
                reads=[ostg_B[op_]], writes=[scro_B[w][h]], queue="poolq")

    build_head(0)
    pendq = []
    gpc = 0
    exp_prev = exp_last = None
    headq = []
    w1_dma(0)
    units = []
    _w2u = [w2_unit_steps(dc) for dc in range(8)]
    for hc in range(32):
        units.append(w1_unit_steps(hc))
        if hc % 4 == 3 and _w2u:
            units.append(_w2u.pop(0))
    units.extend(_w2u)
    w1sq = []
    active = []
    ui = 0
    while True:
        while len(active) < 3 and ui < len(units):
            active.append(iter(units[ui]))
            ui += 1
        if not active:
            break
        it = active.pop(0)
        try:
            w1sq.append(next(it))
            active.append(it)
        except StopIteration:
            pass
    for h in range(H):
        hb = h % 2
        for w in range(NW):
            op_ = (h * NW + w) % 2
            nk = 4 * (w + 1)
            for g in range(0, nk, 2):
                sp2 = gpc % 2
                gp = gpc % 4
                gpc += 1
                cols = []
                for j in range(2):
                    kc = g + j
                    j0 = kc - 4 * w
                    c0 = 128 * j0 if j0 >= 0 else 0
                    cols.append((kc, j, j0, c0))
                    s.op("pe", lambda e, hb=hb, kc=kc, j=j, c0=c0, sp2=sp2,
                         w=w:
                         e.matmul(psg[sp2][:, 512 * j + c0:512 * (j + 1)],
                                  bufK[hb][0:SLOT, P * kc:P * (kc + 1)],
                                  bufQ[hb][0:SLOT, 512 * w + c0:512 * (w + 1)],
                                  start=True, stop=True),
                         reads=[bufK_B[hb], bufQ_B[hb]], writes=[psg_B[sp2][j]])
                if cols[0][2] >= 0:
                    for (kc, j, j0, c0) in cols:
                        er = s.op("act", lambda e, j=j, c0=c0, gp=gp, sp2=sp2:
                                  e.activation(
                                      PT[gp][:, 512 * j + c0:512 * (j + 1)],
                                      psg[sp2][:, 512 * j + c0:512 * (j + 1)],
                                      AF.Exp),
                                  reads=[psg_B[sp2][j]], writes=[PT_B[gp]])
                else:
                    er = s.op("act", lambda e, gp=gp, sp2=sp2: e.activation(
                        PT[gp], psg[sp2], AF.Exp),
                        reads=[psg_B[sp2][0], psg_B[sp2][1]], writes=[PT_B[gp]])
                exp_prev, exp_last = exp_last, er
                for (kc, j, j0, c0) in cols:
                    if j0 >= 0:
                        s.op("dve", lambda e, j=j, c0=c0, gp=gp: e.tensor_mul(
                            PT[gp][:, 512 * j + c0:512 * j + c0 + P],
                            PT[gp][:, 512 * j + c0:512 * j + c0 + P], tril01),
                            reads=[PT_B[gp], tril_B], writes=[PT_B[gp]])
                pendq.append(dict(h=h, w=w, op=op_, cols=cols, gp=gp,
                                  last=(g + 2 >= nk)))
                if len(pendq) > 2:
                    emit_avs(pendq.pop(0))
                if g in (2, 4, 6):
                    if headq:
                        headq.pop(0)()
                    elif w1sq:
                        w1sq.pop(0)()
            if w == 1 and h + 1 < H:
                headq.extend(build_head_steps(h + 1))
            if w == NW - 1:
                while headq:
                    headq.pop(0)()
            if NW == 1 and h + 1 < H:
                build_head(h + 1)
    while w1sq:
        w1sq.pop(0)()
    for p in pendq:
        emit_avs(p)
    s.barrier(queues=("spq", "poolq"))
    if phases == "AB":
        s.final_wait("pool", "poolq")
        s.final_wait("sp", "spq")
        s.replay(nc)
        return nc

    Ofm_B = B_("Ofm"); den_B = B_("den")
    xwin_B = [B_("xwin0"), B_("xwin1")]
    xn2tm_B = B_("xn2tm"); xn2fm_B = [B_("xn2fm0"), B_("xn2fm1")]
    gring_B = [B_(f"g{i}") for i in range(16)]
    sw1t_B = [B_(f"sw1t{i}") for i in range(3)]
    sw2t_B = [B_(f"sw2t{i}") for i in range(2)]
    gtmp_B = [B_(f"gtmp{i}") for i in range(2)]
    sqC_B = [B_(f"sqC{i}") for i in range(2)]
    o2_B = [B_(f"o2_{t}") for t in range(4)]
    xmC_B = [B_(f"xmC{i}") for i in range(2)]
    outst_B = [B_(f"outst{i}") for i in range(2)]
    rows_B = B_("rows")
    abc_B = B_("abc")
    ystg_B = [B_(f"ystg{i}") for i in range(2)]
    bnst_B = [B_(f"bnst{i}") for i in range(2)]
    lncC_B = [B_(f"lncC{i}") for i in range(2)]
    lncD_B = [B_(f"lncD{i}") for i in range(2)]
    cps = [ps_big[:, 512 * i:512 * (i + 1)] for i in range(4)]
    cps_B = [B_(f"cps{i}") for i in range(4)]
    cps2 = ps_big[:, 1024:2048]
    cps2_B = B_("cps2w")

    def c_loads_od(w):
        s.op("pool", lambda e, w=w: e.dma_start(
            out=denom, in_=scr_o[w, :, 64, :]),
            reads=[scro_B[w][hh] for hh in range(H)], writes=[den_B],
            queue="poolq")
        for r in range(2):
            s.op("pool", lambda e, w=w, r=r: e.dma_start(
                out=O_fm[64 * r:64 * r + 64, :, :],
                in_=scr_o[w, r::2, 0:64, :].rearrange("g f q -> f g q")),
                reads=[scro_B[w][hh] for hh in range(r, H, 2)], writes=[Ofm_B],
                queue="poolq")

    def c_loads_x(w):
        wp = w % 2
        s.op("pool", lambda e, w=w, wp=wp: e.dma_start(
            out=x_win[wp],
            in_=x_in[4 * w:4 * w + 4].rearrange("t p d -> p t d")),
            writes=[xwin_B[wp]], queue="poolq")

    def c_loads(w):
        c_loads_od(w)
        c_loads_x(w)

    def c_stageA1(w):
        def _recs(e):
            with nc.allow_low_precision(reason="fp32r PE operand rounding"):
                return e.reciprocal(recS, denom)
        s.op("dve", _recs, reads=[den_B], writes=[recS_B])
        for c_ in range(8):
            pb, pbB = (ps_a, psA_B) if c_ % 2 == 0 else (ps_b, psB_B)
            s.op("pe", lambda e, c_=c_, pb=pb: e.matmul(
                pb, sel_r[:, P * c_:P * (c_ + 1)], recS,
                start=True, stop=True),
                reads=[selr_B, recS_B], writes=[pbB])
            s.op("dve", lambda e, c_=c_, pb=pb: e.tensor_mul(
                O_fm[:, c_, :], O_fm[:, c_, :], pb),
                reads=[Ofm_B, pbB], writes=[Ofm_B])

    def c_stageA2(w):
        for c_ in range(8):
            s.op("pe", lambda e, c_=c_: e.matmul(
                ps_a[0:1, :], ones128, O_fm[:, c_, :],
                start=(c_ == 0), stop=(c_ == 7)),
                reads=[ones_B, Ofm_B], writes=[psA_B], acc=True)
        s.op("dve", lambda e: e.tensor_scalar_mul(murow, ps_a[0:1, :], 1.0 / D),
             reads=[psA_B], writes=[rows_B])
        for c_ in range(8):
            sp_ = c_ % 2
            def _sq(e, c_=c_, sp_=sp_):
                with nc.allow_low_precision(
                        reason="fp32r PE operand rounding"):
                    return e.tensor_mul(
                        sq_sC[sp_], O_fm[:, c_, :], O_fm[:, c_, :])
            s.op("dve", _sq, reads=[Ofm_B], writes=[sqC_B[sp_]])
            s.op("pe", lambda e, c_=c_, sp_=sp_: e.matmul(
                ps_b[0:1, :], ones128r, sq_sC[sp_],
                start=(c_ == 0), stop=(c_ == 7)),
                reads=[onesr_B, sqC_B[sp_]], writes=[psB_B], acc=True)
        s.op("dve", lambda e: e.tensor_scalar_mul(e2row, ps_b[0:1, :], 1.0 / D),
             reads=[psB_B], writes=[rows_B])
        s.op("dve", lambda e: e.tensor_mul(tmprow, murow, murow),
             reads=[rows_B], writes=[rows_B])
        s.op("dve", lambda e: e.tensor_sub(varrow, e2row, tmprow),
             reads=[rows_B], writes=[rows_B])
        s.op("act", lambda e: e.activation(tmprow, varrow, AF.Sqrt,
                                           bias=consts[0:1, 0:1]),
             reads=[rows_B, consts_B], writes=[rows_B])
        s.op("dve", lambda e: e.reciprocal(arow, tmprow),
             reads=[rows_B], writes=[rows_B])
        s.op("dve", lambda e: e.scalar_tensor_tensor(
            out=nbrow, in0=murow, scalar=-1.0, in1=arow,
            op0=ALU.mult, op1=ALU.mult),
            reads=[rows_B], writes=[rows_B])
    def c_stageB1(w):
        wp = w % 2
        for t in range(4):
            tp = t % 2
            s.op("pe", lambda e, t=t, tp=tp: e.transpose(
                PSAB[tp][:, 0:1], arow[:, P * t:P * (t + 1)],
                ident_f[0:1, 0:1]),
                reads=[rows_B, identf_B], writes=[PSAB_B[tp]])
            s.op("pe", lambda e, t=t, tp=tp: e.transpose(
                PSAB[tp][:, 1:2], nbrow[:, P * t:P * (t + 1)],
                ident_f[0:1, 0:1]),
                reads=[rows_B, identf_B], writes=[PSAB_B[tp]])
            s.op("dve", lambda e, t=t, tp=tp: e.tensor_copy(
                abcol[:, 2 * t:2 * t + 2], PSAB[tp][:, 0:2]),
                reads=[PSAB_B[tp]], writes=[abc_B])
        for t in range(4):
            for cb in range(2):
                tp = (t * 2 + cb) % 2
                yp = (t * 2 + cb) % 2
                for c_ in range(4 * cb, 4 * cb + 4):
                    s.op("pe", lambda e, c_=c_, t=t, tp=tp, cb=cb: e.transpose(
                        PSAB[tp][:, P * (c_ - 4 * cb):P * (c_ - 4 * cb + 1)],
                        O_fm[:, c_, P * t:P * (t + 1)], ident_f),
                        reads=[Ofm_B, identf_B], writes=[PSAB_B[tp]],
                        acc=True)
                s.op("dve", lambda e, t=t, tp=tp, yp=yp: e.tensor_scalar(
                    out=ystg[yp], in0=PSAB[tp],
                    scalar1=abcol[:, 2 * t:2 * t + 1],
                    scalar2=abcol[:, 2 * t + 1:2 * t + 2],
                    op0=ALU.mult, op1=ALU.add),
                    reads=[PSAB_B[tp], abc_B], writes=[ystg_B[yp]])
                s.op("pool", lambda e, cb=cb, t=t, yp=yp, wp=wp: e.tensor_add(
                    x_win[wp][:, t, 512 * cb:512 * (cb + 1)],
                    x_win[wp][:, t, 512 * cb:512 * (cb + 1)], ystg[yp]),
                    reads=[ystg_B[yp], xwin_B[wp]], writes=[xwin_B[wp]])
        for t in range(4):
            par, b = t % 2, 8 * (t // 2)
            lc, lcB = lncolC[par], lncC_B[par]
            bs, bsB = bnst[par], bnst_B[par]
            for hf in range(2):
                s.op("dve", lambda e, t=t, bs=bs, hf=hf, wp=wp: e.bn_stats(
                    bs[:, 6 * hf:6 * hf + 6],
                    x_win[wp][:, t, 512 * hf:512 * (hf + 1)]),
                    reads=[xwin_B[wp]], writes=[bsB])
            s.op("dve", lambda e, lc=lc, b=b, bs=bs: e.bn_aggr(
                lc[:, b + 2:b + 4], bs),
                reads=[bsB], writes=[lcB])
        for t in range(4):
            par, b = t % 2, 8 * (t // 2)
            lc, lcB = lncolC[par], lncC_B[par]
            s.op("act", lambda e, lc=lc, b=b: e.activation(
                lc[:, b + 5:b + 6], lc[:, b + 3:b + 4], AF.Sqrt,
                bias=consts[:, 0:1]),
                reads=[lcB, consts_B], writes=[lcB])
        for t in range(4):
            par, b = t % 2, 8 * (t // 2)
            lc, lcB = lncolC[par], lncC_B[par]
            s.op("dve", lambda e, lc=lc, b=b: e.reciprocal(
                lc[:, b + 6:b + 7], lc[:, b + 5:b + 6]),
                reads=[lcB], writes=[lcB])
            s.op("dve", lambda e, lc=lc, b=b: e.scalar_tensor_tensor(
                out=lc[:, b + 7:b + 8], in0=lc[:, b + 2:b + 3],
                scalar=-1.0, in1=lc[:, b + 6:b + 7],
                op0=ALU.mult, op1=ALU.mult),
                reads=[lcB], writes=[lcB])
        for t in range(4):
            par, b = t % 2, 8 * (t // 2)
            lc, lcB = lncolC[par], lncC_B[par]
            s.op("dve", lambda e, t=t, lc=lc, b=b, wp=wp: e.tensor_scalar(
                out=xn2_tm[:, t, :], in0=x_win[wp][:, t, :],
                scalar1=lc[:, b + 6:b + 7], scalar2=lc[:, b + 7:b + 8],
                op0=ALU.mult, op1=ALU.add),
                reads=[xwin_B[wp], lcB], writes=[xn2tm_B])
    def c_stageB2(w):
        wp = w % 2
        for c0 in range(0, 8, 2):
            tb = (c0 // 2) % 2
            for c_ in (c0, c0 + 1):
                for t in range(4):
                    sl = (c_ - c0) * 4 + t
                    s.op("pe", lambda e, c_=c_, t=t, sl=sl, tb=tb: e.transpose(
                        TBF[tb][:, P * sl:P * (sl + 1)],
                        xn2_tm[:, t, P * c_:P * (c_ + 1)], ident_bf),
                        reads=[xn2tm_B, identbf_B], writes=[TBF_B[tb]])
            s.op("dve", lambda e, c0=c0, tb=tb, wp=wp: e.tensor_copy(
                xn2_fm[wp][:, c0:c0 + 2, :].rearrange("p c q -> p (c q)"),
                TBF[tb]),
                reads=[TBF_B[tb]], writes=[xn2fm_B[wp]])

    def c_epi_part1(w, t):
        par, b = t % 2, 8 * (t // 2)
        lc, lcB = lncolD[par], lncD_B[par]
        s.op("dve", lambda e, t=t: e.tensor_add(
            o2_sb[:, t, :], o2_sb[:, t, :], b2b),
            reads=[o2_B[t], b2b_B], writes=[o2_B[t]])
        bs, bsB = bnst[par], bnst_B[par]
        for hf in range(2):
            s.op("dve", lambda e, t=t, bs=bs, hf=hf: e.bn_stats(
                bs[:, 6 * hf:6 * hf + 6],
                o2_sb[:, t, 512 * hf:512 * (hf + 1)]),
                reads=[o2_B[t]], writes=[bsB])
        s.op("dve", lambda e, lc=lc, b=b, bs=bs: e.bn_aggr(
            lc[:, b + 2:b + 4], bs),
            reads=[bsB], writes=[lcB])

    def c_epi_tail(w):
        wp = w % 2
        for t in range(4):
            par, b = t % 2, 8 * (t // 2)
            lc, lcB = lncolD[par], lncD_B[par]
            s.op("act", lambda e, lc=lc, b=b: e.activation(
                lc[:, b + 5:b + 6], lc[:, b + 3:b + 4], AF.Sqrt,
                bias=consts[:, 0:1]),
                reads=[lcB, consts_B], writes=[lcB])
        for t in range(4):
            par, b = t % 2, 8 * (t // 2)
            lc, lcB = lncolD[par], lncD_B[par]
            s.op("dve", lambda e, lc=lc, b=b: e.reciprocal(
                lc[:, b + 6:b + 7], lc[:, b + 5:b + 6]),
                reads=[lcB], writes=[lcB])
            s.op("dve", lambda e, lc=lc, b=b: e.scalar_tensor_tensor(
                out=lc[:, b + 7:b + 8], in0=lc[:, b + 2:b + 3],
                scalar=-1.0, in1=lc[:, b + 6:b + 7],
                op0=ALU.mult, op1=ALU.mult),
                reads=[lcB], writes=[lcB])
        for t in range(4):
            par, b = t % 2, 8 * (t // 2)
            lc, lcB = lncolD[par], lncD_B[par]
            s.op("dve", lambda e, t=t, par=par, lc=lc, b=b: e.tensor_scalar(
                out=o2_sb[:, t, :], in0=o2_sb[:, t, :],
                scalar1=lc[:, b + 6:b + 7], scalar2=lc[:, b + 7:b + 8],
                op0=ALU.mult, op1=ALU.add),
                reads=[o2_B[t], lcB], writes=[o2_B[t]])
            s.op("dve", lambda e, t=t, par=par, wp=wp: e.tensor_add(
                outstage[par], x_win[wp][:, t, :], o2_sb[:, t, :]),
                reads=[xwin_B[wp], o2_B[t]], writes=[outst_B[par]])
            s.op("pool", lambda e, w=w, t=t, par=par: e.dma_start(
                out=out_t[4 * w + t], in_=outstage[par]),
                reads=[outst_B[par]], queue="poolq")

    c_loads(0)
    c_stageA1(0)
    c_stageA2(0)
    c_stageB1(0)
    c_stageB2(0)
    if NW > 1:
        c_loads_od(1)
        c_stageA1(1)
    sw1_next = [0]

    def emit_sw1_load():
        gi = sw1_next[0]
        if gi >= 8 * NW:
            return
        sw1_next[0] += 1
        q = gi % 8
        s.op("sp", lambda e, q=q, gi=gi: e.dma_start(
            out=sw1t_sb[gi % 2],
            in_=scr_sw1t[4 * q:4 * q + 4].rearrange("h p c q -> p h c q")),
            reads=[scr1_B[hc] for hc in range(4 * q, 4 * q + 4)],
            writes=[sw1t_B[gi % 2]], queue="spq")

    emit_sw1_load()
    emit_sw1_load()
    for w in range(NW):
        wp = w % 2
        for grp in range(5):
            gb = grp % 2
            for k in (range(8) if grp < 4 else []):
                hc = 8 * grp + k
                s.op("sp", lambda e, hc=hc, k=k, gb=gb: e.dma_start(
                    out=sw2t_sb[gb][:, k, :].rearrange(
                        "p (d q) -> p d q", q=P),
                    in_=scr_sw2t[:, hc // 16, hc % 16].rearrange(
                        "d p q -> p d q")),
                    reads=[scr2h_B[dc][hc // 16] for dc in range(8)],
                    writes=[sw2t_B[gb]], queue="spq")
            for k in (range(8) if grp < 4 else []):
                hc = 8 * grp + k
                par = (hc // 4) % 2
                slot = hc % 16
                pq = hc % 2
                for c_ in range(8):
                    s.op("pe", lambda e, hc=hc, par=par, c_=c_, pq=pq, wp=wp:
                         e.matmul(
                             cps[pq], sw1t_sb[par][:, hc % 4, c_, :],
                             xn2_fm[wp][:, c_, :],
                             start=(c_ == 0), stop=(c_ == 7)),
                         reads=[sw1t_B[par], xn2fm_B[wp]], writes=[cps_B[pq]],
                         acc=True)
                if k % 4 == 3:
                    emit_sw1_load()
                s.op("act", lambda e, hc=hc, slot=slot, pq=pq: e.activation(
                    g_ring[:, slot, :], cps[pq], AF.Gelu,
                    bias=b1_sb[:, hc:hc + 1], scale=alpha1_sb[:, hc:hc + 1]),
                    reads=[cps_B[pq], b1s_B, a1_B], writes=[gring_B[slot]])
            fgrp = grp - 1
            fgb = fgrp % 2
            for t in (range(4) if fgrp >= 0 else []):
                for dh in range(2):
                    f2 = 2 + (t * 2 + dh) % 2
                    for k in range(8):
                        hc = 8 * fgrp + k
                        slot = hc % 16
                        s.op("pe", lambda e, slot=slot, t=t, dh=dh, k=k,
                             fgb=fgb, f2=f2: e.matmul(
                                 cps[f2], g_ring[:, slot, P * t:P * (t + 1)],
                                 sw2t_sb[fgb][:, k, 512 * dh:512 * (dh + 1)],
                                 start=(k == 0), stop=(k == 7)),
                             reads=[gring_B[slot], sw2t_B[fgb]],
                             writes=[cps_B[f2]], acc=True)
                    if fgrp == 0:
                        s.op("dve", lambda e, t=t, dh=dh, f2=f2: e.tensor_copy(
                            o2_sb[:, t, 512 * dh:512 * (dh + 1)], cps[f2]),
                            reads=[cps_B[f2]], writes=[o2_B[t]])
                    else:
                        s.op("dve", lambda e, t=t, dh=dh, f2=f2: e.tensor_add(
                            o2_sb[:, t, 512 * dh:512 * (dh + 1)],
                            o2_sb[:, t, 512 * dh:512 * (dh + 1)], cps[f2]),
                            reads=[cps_B[f2], o2_B[t]], writes=[o2_B[t]])
                    if fgrp == 3 and dh == 1:
                        c_epi_part1(w, t)
            if grp == 0 and w + 1 < NW:
                c_stageA2(w + 1)
            elif grp == 1 and w + 1 < NW:
                c_loads_x(w + 1)
                c_stageB1(w + 1)
            elif grp == 2:
                if w + 1 < NW:
                    c_stageB2(w + 1)
                if w + 2 < NW:
                    c_loads_od(w + 2)
            elif grp == 3 and w + 2 < NW:
                c_stageA1(w + 2)
        c_epi_tail(w)
    nc._dbg = dict(O_fm=O_fm, denom=denom, x_win=x_win, xn2_tm=xn2_tm,
                   o2_sb=o2_sb, rowt=rowt, M_bf=M_bf, xn_tm=xn_tm)
    s.final_wait("pool", "poolq")
    s.final_wait("sp", "spq")
    s.replay(nc)
    return nc


def kernel(L0_attn_x, L0_ffn_w1, L0_ffn_b1, L0_ffn_w2, L0_ffn_b2):
    S = 4096
    x = np.ascontiguousarray(np.asarray(L0_attn_x, dtype=np.float32))
    w1 = np.ascontiguousarray(np.asarray(L0_ffn_w1, dtype=np.float32)).reshape(32, P, D)
    b1 = np.ascontiguousarray(np.asarray(L0_ffn_b1, dtype=np.float32)).reshape(32, P, 1)
    w2 = np.ascontiguousarray(np.asarray(L0_ffn_w2, dtype=np.float32)).reshape(8, P, HFF)
    b2 = np.ascontiguousarray(np.asarray(L0_ffn_b2, dtype=np.float32)).reshape(1, D)
    nc = build_program(S)
    in_maps = [{"x_in": np.ascontiguousarray(x[b].reshape(S // P, P, D)),
                "w1_in": w1, "b1_in": b1, "w2_in": w2, "b2_in": b2}
               for b in range(8)]
    res = run_bass_kernel_spmd(nc, in_maps, core_ids=list(range(8)))
    return np.stack([np.asarray(r["out"]).reshape(S, D) for r in res.results],
                    axis=0)


# version 34 (final)
# speedup vs baseline: 1.0599x; 1.0133x over previous
import numpy as np
from contextlib import ExitStack
import concourse.bass as bass
import concourse.mybir as mybir
from concourse import bass_utils as _bass_utils
from concourse.bass_utils import run_bass_kernel_spmd

if not getattr(_bass_utils, "_ldw_patch", False):
    _orig_run_command = _bass_utils.run_command

    def _run_command_ldw(argv, **kwargs):
        try:
            argv = ["--enable-ldw-opt=true" if a == "--enable-ldw-opt=false"
                    else a for a in argv]
        except TypeError:
            pass
        return _orig_run_command(argv, **kwargs)

    _bass_utils.run_command = _run_command_ldw
    _bass_utils._ldw_patch = True

F32 = mybir.dt.float32
F32R = mybir.dt.float32r
BF16 = mybir.dt.bfloat16
U8 = mybir.dt.uint8


def _r(ap):
    return ap.bitcast(F32R)
AF = mybir.ActivationFunctionType
ALU = mybir.AluOpType
AX = mybir.AxisListType

P = 128
D = 1024
H = 16
HD = 64
HFF = 4096
EPS = 1e-5
SLOT = 65

COMPUTE_ENGINES = ("pe", "act", "dve", "pool", "sp")


class Buf:
    __slots__ = ("name", "last_w", "readers")

    def __init__(self, name):
        self.name = name
        self.last_w = None
        self.readers = []


class OpRec:
    __slots__ = ("engine", "fn", "wait_ops", "extra_waits", "needs_inc",
                 "milestone", "qsem", "idx", "tag", "acc")

    def __init__(self, engine, fn, qsem=None, tag=""):
        self.acc = False
        self.engine = engine
        self.fn = fn
        self.wait_ops = []
        self.extra_waits = []
        self.needs_inc = False
        self.milestone = None
        self.qsem = qsem
        self.idx = 0
        self.tag = tag


class Sched:
    NSEM = 8

    def __init__(self, same_engine_edges=COMPUTE_ENGINES):
        self.ops = {e: [] for e in COMPUTE_ENGINES}
        self.queue_idx = {}
        self.qsem_total = {}
        self.waited_idx = {e: {} for e in COMPUTE_ENGINES}
        self.waited_q = {e: {} for e in COMPUTE_ENGINES}
        self.same_engine_edges = set(same_engine_edges)
        self.n_ops = 0

    def _dep_wait(self, rec, dep):
        eng = rec.engine
        if dep.qsem is not None:
            wm = self.waited_q[eng]
            if wm.get(dep.qsem, 0) < dep.milestone:
                wm[dep.qsem] = dep.milestone
                rec.extra_waits.append((dep.qsem, dep.milestone))
            return
        if (dep.engine == eng and rec.qsem is None
                and (eng not in self.same_engine_edges
                     or (rec.acc and dep.acc and eng == "pe"))):
            return
        wm = self.waited_idx[eng]
        if wm.get(dep.engine, -1) >= dep.idx:
            return
        wm[dep.engine] = dep.idx
        dep.needs_inc = True
        rec.wait_ops.append(dep)

    def op(self, engine, fn, reads=(), writes=(), queue=None, tag="",
           acc=False, dep_ops=()):
        qsem = None
        if queue is not None:
            i = self.queue_idx.get(queue, 0)
            self.queue_idx[queue] = i + 1
            qsem = f"{queue}{i % self.NSEM}"
        rec = OpRec(engine, fn, qsem=qsem, tag=tag)
        rec.acc = acc
        rec.idx = len(self.ops[engine])
        deps = list(dep_ops)
        for b in reads:
            if b.last_w is not None:
                deps.append(b.last_w)
        for b in writes:
            if b.last_w is not None and not b.readers:
                deps.append(b.last_w)
            deps.extend(b.readers)
        seen = set()
        for d in deps:
            if d is rec or id(d) in seen:
                continue
            seen.add(id(d))
            self._dep_wait(rec, d)
        if qsem is not None:
            prev = self.qsem_total.get(qsem, 0)
            if prev:
                wm = self.waited_q[engine]
                if wm.get(qsem, 0) < prev:
                    wm[qsem] = prev
                    rec.extra_waits.append((qsem, prev))
            rec.milestone = prev + 16
            self.qsem_total[qsem] = rec.milestone
        self.ops[engine].append(rec)
        self.n_ops += 1
        for b in writes:
            b.last_w = rec
            b.readers = []
        for b in reads:
            b.readers.append(rec)
        return rec

    def barrier(self, engines=COMPUTE_ENGINES, queues=()):
        targets = [self.ops[e][-1] for e in engines if self.ops[e]]
        qtargets = []
        for q in queues:
            for i in range(self.NSEM):
                tot = self.qsem_total.get(f"{q}{i}", 0)
                if tot:
                    qtargets.append((f"{q}{i}", tot))
        for e in engines:
            rec = OpRec(e, None, tag="barrier")
            rec.idx = len(self.ops[e])
            for t in targets:
                if t.engine == e and t.qsem is None:
                    continue
                self._dep_wait(rec, t)
            wm = self.waited_q[e]
            for sname, val in qtargets:
                if wm.get(sname, 0) < val:
                    wm[sname] = val
                    rec.extra_waits.append((sname, val))
            if rec.wait_ops or rec.extra_waits:
                self.ops[e].append(rec)

    def final_wait(self, engine, queue):
        rec = OpRec(engine, None, tag="final")
        rec.idx = len(self.ops[engine])
        wm = self.waited_q[engine]
        for i in range(self.NSEM):
            sname = f"{queue}{i}"
            tot = self.qsem_total.get(sname, 0)
            if tot and wm.get(sname, 0) < tot:
                wm[sname] = tot
                rec.extra_waits.append((sname, tot))
        if rec.extra_waits:
            self.ops[engine].append(rec)

    def finalize(self):
        for e in COMPUTE_ENGINES:
            count = 0
            for rec in self.ops[e]:
                if rec.qsem is None and rec.needs_inc:
                    count += 1
                    rec.milestone = count

    def replay(self, nc):
        self.finalize()
        semnames = set(self.qsem_total.keys())
        for e in COMPUTE_ENGINES:
            for rec in self.ops[e]:
                if rec.qsem is None and rec.needs_inc:
                    semnames.add(rec.engine)
        with ExitStack() as stack:
            sems = {s: stack.enter_context(nc.semaphore(s))
                    for s in sorted(semnames)}
            block = stack.enter_context(nc.Block())

            def runner(recs):
                def run(eng):
                    for rec in recs:
                        for dep in rec.wait_ops:
                            eng.wait_ge(sems[dep.engine], dep.milestone)
                        for sem, val in rec.extra_waits:
                            eng.wait_ge(sems[sem], val)
                        if rec.fn is None:
                            if rec.needs_inc and rec.qsem is None:
                                eng.nop().then_inc(sems[rec.engine], 1)
                            continue
                        inst = rec.fn(eng)
                        if rec.qsem is not None:
                            inst.then_inc(sems[rec.qsem], 16)
                        elif rec.needs_inc:
                            inst.then_inc(sems[rec.engine], 1)
                return run

            engmap = {"pe": block.tensor, "act": block.scalar,
                      "dve": block.vector, "pool": block.gpsimd,
                      "sp": block.sync}
            for e in COMPUTE_ENGINES:
                if self.ops[e]:
                    engmap[e](runner(self.ops[e]))


def build_program(S=4096, phases="ABC"):
    NW = S // 512
    KCN = S // 128
    nc = bass.Bass("TRN2", target_bir_lowering=False)
    s = Sched()

    x_in = nc.dram_tensor("x_in", [KCN, P, D], F32, kind="ExternalInput").ap()
    w1_in = nc.dram_tensor("w1_in", [32, P, D], F32, kind="ExternalInput").ap()
    b1_in = nc.dram_tensor("b1_in", [32, P, 1], F32, kind="ExternalInput").ap()
    w2_in = nc.dram_tensor("w2_in", [8, P, HFF], F32, kind="ExternalInput").ap()
    b2_in = nc.dram_tensor("b2_in", [1, D], F32, kind="ExternalInput").ap()
    out_t = nc.dram_tensor("out", [KCN, P, D], F32, kind="ExternalOutput").ap()
    scr_sw1t = nc.dram_tensor("scr_sw1t", [32, P, 8, P], BF16, kind="Internal").ap()
    scr_sw2t = nc.dram_tensor("scr_sw2t", [8, 2, 16, P, P], BF16,
                              kind="Internal").ap()
    scr_o = nc.dram_tensor("scr_o", [NW, H, SLOT, 512], F32, kind="Internal").ap()
    scr_a2 = nc.dram_tensor("scr_a2", [8, P, 1], F32, kind="Internal").ap()

    def sball(name, shape, dtype):
        return nc.alloc_sbuf_tensor(name, shape, dtype).ap()

    consts = sball("consts", [P, 8], F32)
    consts16 = sball("consts16", [16, 1], F32)
    ident_bf = sball("ident_bf", [P, P], BF16)
    ident_f = sball("ident_f", [P, P], F32)
    tril01 = sball("tril01", [P, P], BF16)
    sel = sball("sel", [16, 8 * P], F32)
    sel_r = sball("sel_r", [16, 8 * P], F32R)
    ones128r = sball("ones128r", [P, 1], F32R)
    ones128 = sball("ones128", [P, 1], F32)
    ones1x128 = sball("ones1x128", [1, P], F32)
    alpha1_sb = sball("alpha1_sb", [P, 32], F32)
    b1_sb = sball("b1_sb", [P, 32], F32)
    alpha2c = sball("alpha2c", [P, 8], F32)
    alpha2b = sball("alpha2b", [P, D], F32)
    b2b = sball("b2b", [P, D], F32)
    a2row = sball("a2row", [1, D], F32)
    b2row = sball("b2row", [1, D], F32)

    ps_big = nc.alloc_psum_tensor("ps_big", [P, 2048], F32).ap()
    ps_a = nc.alloc_psum_tensor("ps_a", [P, 512], F32).ap()
    ps_b = nc.alloc_psum_tensor("ps_b", [P, 512], F32).ap()
    ps_tbf = nc.alloc_psum_tensor("ps_tbf", [P, 1024], BF16).ap()
    ps_tbf2 = nc.alloc_psum_tensor("ps_tbf2", [P, 1024], BF16).ap()

    arena_h = nc.alloc_sbuf_tensor("arena", [P, nc.sbuf_bytes_remaining - 256], U8)
    base = nc.lookup_mloc(arena_h).addr
    _uid = [0]

    def at(shape, dtype, off):
        _uid[0] += 1
        h = nc.alloc_sbuf_tensor_at(f"ar{_uid[0]}", shape, dtype,
                                    offset=base + off)
        return h.ap()

    def width(shape, dtype):
        n = 1
        for d_ in shape[1:]:
            n *= d_
        return n * mybir.dt.size(dtype)

    xn_tm = at([P, KCN, H * SLOT], BF16, 0)
    o_xn = width([P, KCN, H * SLOT], BF16)
    M_bf = at([16, S], BF16, o_xn)
    AS = o_xn + width([16, S], BF16) + 64
    AS = (AS + 63) & ~63

    oAX = AS
    xraw = [at([P, D], F32, oAX + i * 4096) for i in range(4)]
    sq_sA = [at([P, D], F32, oAX + 16384 + i * 4096) for i in range(4)]
    normq = at([P, KCN, 16], F32, oAX + 32768)
    lncolA = [at([P, 16], F32, oAX + 34816 + i * 64) for i in range(4)]
    bnA = [at([P, 12], F32, oAX + 35072 + i * 64) for i in range(4)]
    bufK = [at([SLOT, S], BF16, AS + i * (S * 2)) for i in range(2)]
    oBK = 2 * S * 2
    bufQ = [at([SLOT, S], BF16, AS + oBK + i * (S * 2)) for i in range(2)]
    oBQ = oBK + 2 * S * 2
    PT = [at([P, 1024], BF16, AS + oBQ + i * 2048) for i in range(4)]
    ostg = [at([SLOT, 512], F32, AS + oBQ + 8192 + i * 2048) for i in range(2)]
    oBW = oBQ + 12288
    w1raw = [at([P, D], F32, AS + oBW + i * 4096) for i in range(4)]
    sw1sb = [at([P, D], BF16, AS + oBW + 16384 + i * 2048) for i in range(4)]
    stg1 = [at([P, 8, P], BF16, AS + oBW + 24576 + i * 2048) for i in range(4)]
    oW2 = oBW + 32768
    w2raw_h = [at([P, 2048], F32, AS + oW2 + i * 8192) for i in range(2)]
    sw2sb_h = [at([P, 2048], BF16, AS + oW2 + 16384 + i * 4096) for i in range(2)]
    stg2dh = [at([P, 16, P], BF16, AS + oW2 + 24576 + i * 4096) for i in range(2)]
    a2part = at([P, 2], F32, AS + oW2 + 32768)

    c = 0
    O_fm = at([P, 8, 512], F32, c); c += width([P, 8, 512], F32)
    denom = at([16, 512], F32, c); c += 2048
    x_win = [at([P, 4, D], F32, c + i * 16384) for i in range(2)]; c += 32768
    xn2_tm = at([P, 4, D], BF16, c); c += width([P, 4, D], BF16)
    xn2_fm = [at([P, 8, 512], BF16, c + i * 8192) for i in range(2)]; c += 16384
    g_ring = at([P, 16, 512], BF16, c); c += width([P, 16, 512], BF16)
    sw1t_sb = [at([P, 4, 8, P], BF16, c + i * 8192) for i in range(2)]; c += 16384
    sw2t_sb = [at([P, 8, D], BF16, c + i * 16384) for i in range(2)]; c += 32768
    sq_sC = [at([P, 512], F32R, c + i * 2048) for i in range(2)]; c += 4096
    o2_sb = at([P, 4, D], F32, c); c += width([P, 4, D], F32)
    outstage = [at([P, D], F32, c + i * 4096) for i in range(2)]; c += 8192
    rowt = [at([1, 512], F32, c + i * 2048) for i in range(4)]; c += 8192
    lncolC = [at([P, 16], F32, c + i * 64) for i in range(2)]; c += 128
    lncolD = [at([P, 16], F32, c + i * 64) for i in range(2)]; c += 128
    abcol = at([P, 8], F32, c); c += 32
    ystg = [at([P, 512], F32, c + i * 2048) for i in range(2)]; c += 4096
    bnst = [at([P, 12], F32, c + i * 64) for i in range(2)]; c += 128
    recS = at([16, 512], F32R, c); c += 2048
    murow, e2row, arow, tmprow = [r[0:1, :] for r in rowt]
    varrow = e2row
    nbrow = tmprow
    B_ = Buf
    consts_B = B_("consts"); consts16_B = B_("c16")
    identbf_B = B_("identbf"); identf_B = B_("identf"); tril_B = B_("tril")
    sel_B = B_("sel"); ones_B = B_("ones"); ones1_B = B_("ones1")
    selr_B = B_("selr"); onesr_B = B_("onesr"); recS_B = B_("recS")
    a1_B = B_("a1"); b1s_B = B_("b1s"); a2c_B = B_("a2c")
    a2b_B = B_("a2b"); b2b_B = B_("b2b"); a2r_B = B_("a2r"); b2r_B = B_("b2r")
    scra2_B = B_("scra2")
    psA_B = B_("psA"); psB_B = B_("psB"); tbf_B = B_("tbf"); tbf2_B = B_("tbf2")
    xn_B = B_("xn"); M_B = B_("M")
    w1raw_B = [B_(f"w1r{i}") for i in range(4)]
    sw1sb_B = [B_(f"sw1s{i}") for i in range(4)]
    stg1_B = [B_(f"stg1{i}") for i in range(4)]
    scr1_B = [B_(f"scr1_{hc}") for hc in range(32)]
    w2rawh_B = [B_(f"w2rh{i}") for i in range(2)]
    sw2sbh_B = [B_(f"sw2sh{i}") for i in range(2)]
    stg2dh_B = [B_(f"stg2dh{i}") for i in range(2)]
    a2p_B = B_("a2p")
    scr2h_B = [[B_(f"scr2_{dc}_{h}") for h in range(2)] for dc in range(8)]
    xraw_B = [B_(f"xr{i}") for i in range(4)]
    sqA_B = [B_(f"sqA{i}") for i in range(4)]
    normq_B = B_("normq")
    lncA_B = [B_(f"lncA{i}") for i in range(4)]
    bnA_B = [B_(f"bnA{i}") for i in range(4)]

    TBF = [ps_tbf, ps_tbf2]
    TBF_B = [tbf_B, tbf2_B]
    T16 = [ps_tbf[:, P * k:P * (k + 1)] for k in range(8)] + \
          [ps_tbf2[:, P * k:P * (k + 1)] for k in range(8)]
    T16_B = [B_(f"t16_{k}") for k in range(16)]
    PSAB = [ps_a, ps_b]
    PSAB_B = [psA_B, psB_B]

    for col, val in [(0, EPS), (1, 1.0 / D), (2, -1.0 / D)]:
        s.op("dve", lambda e, col=col, val=val: e.memset(consts[:, col:col + 1], val),
             writes=[consts_B])
    s.op("dve", lambda e: e.memset(consts16, 16.0), writes=[consts16_B])
    s.op("dve", lambda e: e.memset(ones128, 1.0), writes=[ones_B])
    s.op("dve", lambda e: e.memset(ones1x128, 1.0), writes=[ones1_B])
    s.op("pool", lambda e: e.memset(ident_bf, 0.0), writes=[identbf_B])
    s.op("pool", lambda e: e.affine_select(
        out=ident_bf, in_=ident_bf, compare_op=ALU.not_equal, fill=1.0,
        base=0, pattern=[[-1, P]], channel_multiplier=1),
        reads=[identbf_B], writes=[identbf_B])
    s.op("pool", lambda e: e.memset(ident_f, 0.0), writes=[identf_B])
    s.op("pool", lambda e: e.affine_select(
        out=ident_f, in_=ident_f, compare_op=ALU.not_equal, fill=1.0,
        base=0, pattern=[[-1, P]], channel_multiplier=1),
        reads=[identf_B], writes=[identf_B])
    s.op("pool", lambda e: e.memset(tril01, 1.0), writes=[tril_B])
    s.op("pool", lambda e: e.affine_select(
        out=tril01, in_=tril01, compare_op=ALU.is_ge, fill=0.0,
        base=0, pattern=[[1, P]], channel_multiplier=-1),
        reads=[tril_B], writes=[tril_B])
    s.op("pool", lambda e: e.memset(sel, 1.0), writes=[sel_B])
    s.op("pool", lambda e: e.affine_select(
        out=sel, in_=sel, compare_op=ALU.is_ge, fill=0.0,
        base=0, pattern=[[1, 8 * P]], channel_multiplier=-64),
        reads=[sel_B], writes=[sel_B])
    s.op("pool", lambda e: e.affine_select(
        out=sel, in_=sel, compare_op=ALU.is_ge, fill=0.0,
        base=63, pattern=[[-1, 8 * P]], channel_multiplier=64),
        reads=[sel_B], writes=[sel_B])
    def _selr(e):
        with nc.allow_low_precision(reason="fp32r PE operand rounding"):
            e.tensor_copy(sel_r, sel)
            return e.tensor_copy(ones128r, ones128)
    s.op("dve", _selr, reads=[sel_B, ones_B], writes=[selr_B, onesr_B])

    def x_stage1(tc):
        par = tc % 4
        lc, lcB = lncolA[par], lncA_B[par]
        bs, bsB = bnA[par], bnA_B[par]
        s.op("sp", lambda e, tc=tc, par=par: e.dma_start(
            out=xraw[par], in_=x_in[tc]), writes=[xraw_B[par]], queue="spq")
        xr, xrB = xraw[par], xraw_B[par]
        for hf in range(2):
            s.op("dve", lambda e, xr=xr, bs=bs, hf=hf: e.bn_stats(
                bs[:, 6 * hf:6 * hf + 6], xr[:, 512 * hf:512 * (hf + 1)]),
                reads=[xrB], writes=[bsB])
        s.op("dve", lambda e, lc=lc, bs=bs: e.bn_aggr(lc[:, 0:2], bs),
             reads=[bsB], writes=[lcB])
        s.op("dve", lambda e, lc=lc: e.tensor_scalar_mul(
            lc[:, 4:5], lc[:, 0:1], -1.0), reads=[lcB], writes=[lcB])
        s.op("act", lambda e, lc=lc: e.activation(
            lc[:, 2:3], lc[:, 1:2], AF.Sqrt, bias=consts[:, 0:1]),
            reads=[lcB, consts_B], writes=[lcB])

    def x_stage2(tc):
        par = tc % 4
        lc, lcB = lncolA[par], lncA_B[par]
        xr, xrB = xraw[par], xraw_B[par]
        xr3 = xr.rearrange("p (h c) -> p h c", c=HD)
        s.op("dve", lambda e, lc=lc: e.reciprocal(lc[:, 3:4], lc[:, 2:3]),
             reads=[lcB], writes=[lcB])
        xn3 = xn_tm[:, tc, :].rearrange("p (h c) -> p h c", c=SLOT)
        s.op("dve", lambda e, xr3=xr3, xn3=xn3, lc=lc: e.tensor_scalar(
            out=xn3[:, :, 0:HD], in0=xr3, scalar1=lc[:, 4:5],
            scalar2=lc[:, 3:4], op0=ALU.add, op1=ALU.mult),
            reads=[xrB, lcB], writes=[xn_B])
        s.op("pool", lambda e, xn3=xn3: e.memset(xn3[:, :, HD:SLOT], 1.0),
             writes=[xn_B])
        sq3 = sq_sA[par].rearrange("p (h c) -> p h c", c=HD)
        s.op("pool", lambda e, xn3=xn3, sq3=sq3: e.tensor_mul(
            sq3, xn3[:, :, 0:HD], xn3[:, :, 0:HD]),
            reads=[xn_B], writes=[sqA_B[par]])

    def x_stage3(tc):
        par = tc % 4
        tp = tc % 2
        sq3 = sq_sA[par].rearrange("p (h c) -> p h c", c=HD)
        s.op("dve", lambda e, tc=tc, sq3=sq3, par=par: e.reduce_sum(
            normq[:, tc, :], sq3, axis=AX.X),
            reads=[sqA_B[par]], writes=[normq_B])
        s.op("pe", lambda e, tc=tc, tp=tp: e.transpose(
            PSAB[tp][0:16, 0:P], normq[:, tc, :], ident_f),
            reads=[normq_B, identf_B], writes=[PSAB_B[tp]])
        s.op("act", lambda e, tc=tc, tp=tp: e.activation(
            M_bf[:, P * tc:P * (tc + 1)], PSAB[tp][0:16, 0:P], AF.Sqrt,
            scale=consts16),
            reads=[PSAB_B[tp], consts16_B], writes=[M_B])

    def w2_dma(dc, h):
        s.op("sp", lambda e, dc=dc, h=h: e.dma_start(
            out=w2raw_h[h], in_=w2_in[dc][:, 2048 * h:2048 * (h + 1)]),
            writes=[w2rawh_B[h]], queue="spq")

    def w2_unit_steps(dc):
        steps = []

        def s0(dc=dc):
            w2_dma(dc, 0)

        def s1(dc=dc):
            w2_dma(dc, 1)
            s.op("act", lambda e: e.activation(
                sw2sb_h[0], w2raw_h[0], AF.Sign),
                reads=[w2rawh_B[0]], writes=[sw2sbh_B[0]])
            s.op("dve", lambda e: e.reduce_sum(
                a2part[:, 0:1], w2raw_h[0], axis=AX.X,
                apply_absolute_value=True),
                reads=[w2rawh_B[0]], writes=[a2p_B])

        def s2(dc=dc):
            s.op("act", lambda e: e.activation(
                sw2sb_h[1], w2raw_h[1], AF.Sign),
                reads=[w2rawh_B[1]], writes=[sw2sbh_B[1]])
            s.op("dve", lambda e: e.reduce_sum(
                a2part[:, 1:2], w2raw_h[1], axis=AX.X,
                apply_absolute_value=True),
                reads=[w2rawh_B[1]], writes=[a2p_B])
            s.op("dve", lambda e, dc=dc: e.tensor_add(
                alpha2c[:, dc:dc + 1], a2part[:, 0:1], a2part[:, 1:2]),
                reads=[a2p_B], writes=[a2c_B])
            s.op("dve", lambda e, dc=dc: e.tensor_scalar_mul(
                alpha2c[:, dc:dc + 1], alpha2c[:, dc:dc + 1], 1.0 / HFF),
                reads=[a2c_B], writes=[a2c_B])
            for h in range(2):
                s.op("dve", lambda e, dc=dc, h=h: e.tensor_scalar_mul(
                    sw2sb_h[h], sw2sb_h[h], alpha2c[:, dc:dc + 1]),
                    reads=[sw2sbh_B[h], a2c_B], writes=[sw2sbh_B[h]])
        steps += [s0, s1, s2]
        for h in range(2):
            for g0 in (0, 8):
                def stw(dc=dc, h=h, g0=g0):
                    tb = (g0 // 8) % 2
                    for i in range(g0, g0 + 8):
                        s.op("pe", lambda e, h=h, i=i, g0=g0, tb=tb:
                             e.transpose(
                                 TBF[tb][:, P * (i - g0):P * (i - g0 + 1)],
                                 sw2sb_h[h][:, P * i:P * (i + 1)], ident_bf),
                             reads=[sw2sbh_B[h], identbf_B],
                             writes=[TBF_B[tb]])
                    s.op("dve", lambda e, h=h, g0=g0, tb=tb: e.tensor_copy(
                        stg2dh[h][:, g0:g0 + 8, :].rearrange(
                            "p c q -> p (c q)"), TBF[tb]),
                        reads=[TBF_B[tb]], writes=[stg2dh_B[h]])
                steps.append(stw)

                def stc(dc=dc, h=h, g0=g0):
                    if g0 == 8:
                        s.op("pool", lambda e, dc=dc, h=h: e.dma_start(
                            out=scr_sw2t[dc, h].rearrange("h p q -> p h q"),
                            in_=stg2dh[h]),
                            reads=[stg2dh_B[h]], writes=[scr2h_B[dc][h]],
                            queue="poolq")
                steps.append(stc)
        return steps

    def w1_dma(hc):
        par = hc % 4
        s.op("sp", lambda e, hc=hc, par=par: e.dma_start(
            out=w1raw[par], in_=w1_in[hc]), writes=[w1raw_B[par]], queue="spq")
        s.op("sp", lambda e, hc=hc: e.dma_start(
            out=b1_sb[:, hc:hc + 1], in_=b1_in[hc]), writes=[b1s_B],
            queue="spq")

    def w1_unit_steps(hc):
        par = hc % 4
        steps = []

        def s0(hc=hc, par=par):
            if hc + 1 < 32:
                w1_dma(hc + 1)
            s.op("act", lambda e, par=par: e.activation(
                sw1sb[par], w1raw[par], AF.Sign),
                reads=[w1raw_B[par]], writes=[sw1sb_B[par]])
            s.op("dve", lambda e, hc=hc, par=par: e.reduce_sum(
                alpha1_sb[:, hc:hc + 1], w1raw[par], axis=AX.X,
                apply_absolute_value=True),
                reads=[w1raw_B[par]], writes=[a1_B])
            s.op("dve", lambda e, hc=hc: e.tensor_scalar_mul(
                alpha1_sb[:, hc:hc + 1], alpha1_sb[:, hc:hc + 1], 1.0 / D),
                reads=[a1_B], writes=[a1_B])
        steps.append(s0)
        def st1(hc=hc, par=par):
            tb = hc % 2
            for c_ in range(8):
                s.op("pe", lambda e, par=par, c_=c_, tb=tb: e.transpose(
                    TBF[tb][:, P * c_:P * (c_ + 1)],
                    sw1sb[par][:, P * c_:P * (c_ + 1)], ident_bf),
                    reads=[sw1sb_B[par], identbf_B], writes=[TBF_B[tb]])
            s.op("dve", lambda e, par=par, tb=tb: e.tensor_copy(
                stg1[par].rearrange("p c q -> p (c q)"), TBF[tb]),
                reads=[TBF_B[tb]], writes=[stg1_B[par]])

        def st2(hc=hc, par=par):
            s.op("pool", lambda e, hc=hc, par=par: e.dma_start(
                out=scr_sw1t[hc], in_=stg1[par]),
                reads=[stg1_B[par]], writes=[scr1_B[hc]],
                queue="poolq")
        steps.append(st1)
        steps.append(st2)
        return steps

    for tc in range(KCN + 4):
        if tc < KCN:
            x_stage1(tc)
        if 2 <= tc < KCN + 2:
            x_stage2(tc - 2)
        if tc >= 4:
            x_stage3(tc - 4)
    s.op("sp", lambda e: e.dma_start(out=b2row, in_=b2_in),
         writes=[b2r_B], queue="spq")
    for half in range(2):
        sl = slice(512 * half, 512 * (half + 1))
        s.op("pe", lambda e, sl=sl: e.matmul(
            ps_b, ones1x128, b2row[:, sl], start=True, stop=True),
            reads=[ones1_B, b2r_B], writes=[psB_B])
        s.op("dve", lambda e, sl=sl: e.tensor_copy(b2b[:, sl], ps_b),
             reads=[psB_B], writes=[b2b_B])
    s.barrier(queues=("spq", "poolq"))
    if phases == "A":
        s.final_wait("pool", "poolq")
        s.final_wait("sp", "spq")
        s.replay(nc)
        return nc

    bufK_B = [B_(f"bufK{i}") for i in range(2)]
    bufQ_B = [B_(f"bufQ{i}") for i in range(2)]
    PT_B = [B_(f"PT{i}") for i in range(4)]
    ostg_B = [B_(f"ostg{i}") for i in range(2)]
    psg = [ps_big[:, 0:1024], ps_big[:, 1024:2048]]
    psg_B = [[B_("psg00"), B_("psg01")], [B_("psg10"), B_("psg11")]]
    pso_B = [psA_B, psB_B]
    scro_B = [[B_(f"scro{w}_{h}") for h in range(H)] for w in range(NW)]

    def build_head_steps(h):
        hb = h % 2
        steps = []

        def s0(hb=hb):
            s.op("pool", lambda e, hb=hb: e.memset(bufK[hb][64:SLOT, :], -1.0),
                 writes=[bufK_B[hb]])
        steps.append(s0)
        for kc0 in range(0, KCN, 8):
            def sk(h=h, hb=hb, kc0=kc0):
                tb = (kc0 // 8) % 2
                ke = min(kc0 + 8, KCN)
                for kc in range(kc0, ke):
                    s.op("pe", lambda e, h=h, kc=kc, kc0=kc0, tb=tb:
                         e.transpose(
                             TBF[tb][0:HD, P * (kc - kc0):P * (kc - kc0 + 1)],
                             xn_tm[:, kc, SLOT * h:SLOT * h + HD], ident_bf),
                         reads=[xn_B, identbf_B], writes=[TBF_B[tb]])
                s.op("dve", lambda e, hb=hb, kc0=kc0, ke=ke, tb=tb:
                     e.tensor_copy(
                         bufK[hb][0:HD, P * kc0:P * ke],
                         TBF[tb][0:HD, 0:P * (ke - kc0)]),
                     reads=[TBF_B[tb]], writes=[bufK_B[hb]])
            steps.append(sk)

        def s1(hb=hb, h=h):
            s.op("dve", lambda e, hb=hb: e.tensor_scalar_mul(
                bufQ[hb][0:HD, :], bufK[hb][0:HD, :], 0.125),
                reads=[bufK_B[hb]], writes=[bufQ_B[hb]])
            s.op("pool", lambda e, hb=hb, h=h: e.dma_start(
                out=bufQ[hb][HD:SLOT, :], in_=M_bf[h:h + 1, :]),
                reads=[M_B], writes=[bufQ_B[hb]], queue="poolq")
        steps.append(s1)
        return steps

    def build_head(h):
        for st_ in build_head_steps(h):
            st_()

    def emit_avs(p, absorb=None):
        h, w, op_, cols, gp_, last = (p["h"], p["w"], p["op"], p["cols"],
                                      p["gp"], p["last"])
        nk = 4 * (w + 1)
        pso = PSAB[op_]
        for ci, (kc, j, j0, c0) in enumerate(cols):
            dep = [absorb] if (absorb is not None and ci == len(cols) - 1) \
                else []
            s.op("pe", lambda e, h=h, kc=kc, j=j, c0=c0, gp=gp_, pso=pso,
                 st=(kc == 0), sp_=(kc == nk - 1):
                 e.matmul(pso[0:SLOT, c0:512],
                          xn_tm[:, kc, SLOT * h:SLOT * (h + 1)],
                          PT[gp][:, 512 * j + c0:512 * (j + 1)],
                          start=st, stop=sp_),
                 reads=[xn_B, PT_B[gp_]], writes=[pso_B[op_]], acc=True,
                 dep_ops=dep)
        if last:
            s.op("dve", lambda e, op_=op_: e.tensor_copy(
                ostg[op_], PSAB[op_][0:SLOT, :]),
                reads=[pso_B[op_]], writes=[ostg_B[op_]])
            s.op("pool", lambda e, w=w, h=h, op_=op_: e.dma_start(
                out=scr_o[w, h], in_=ostg[op_]),
                reads=[ostg_B[op_]], writes=[scro_B[w][h]], queue="poolq")

    build_head(0)
    pendq = []
    gpc = 0
    exp_prev = exp_last = None
    headq = []
    w1_dma(0)
    units = []
    _w2u = [w2_unit_steps(dc) for dc in range(8)]
    for hc in range(32):
        units.append(w1_unit_steps(hc))
        if hc % 4 == 3 and _w2u:
            units.append(_w2u.pop(0))
    units.extend(_w2u)
    w1sq = []
    active = []
    ui = 0
    while True:
        while len(active) < 3 and ui < len(units):
            active.append(iter(units[ui]))
            ui += 1
        if not active:
            break
        it = active.pop(0)
        try:
            w1sq.append(next(it))
            active.append(it)
        except StopIteration:
            pass
    for h in range(H):
        hb = h % 2
        for w in range(NW):
            op_ = (h * NW + w) % 2
            nk = 4 * (w + 1)
            for g in range(0, nk, 2):
                sp2 = gpc % 2
                gp = gpc % 4
                gpc += 1
                cols = []
                for j in range(2):
                    kc = g + j
                    j0 = kc - 4 * w
                    c0 = 128 * j0 if j0 >= 0 else 0
                    cols.append((kc, j, j0, c0))
                    s.op("pe", lambda e, hb=hb, kc=kc, j=j, c0=c0, sp2=sp2,
                         w=w:
                         e.matmul(psg[sp2][:, 512 * j + c0:512 * (j + 1)],
                                  bufK[hb][0:SLOT, P * kc:P * (kc + 1)],
                                  bufQ[hb][0:SLOT, 512 * w + c0:512 * (w + 1)],
                                  start=True, stop=True),
                         reads=[bufK_B[hb], bufQ_B[hb]], writes=[psg_B[sp2][j]])
                if cols[0][2] >= 0:
                    for (kc, j, j0, c0) in cols:
                        er = s.op("act", lambda e, j=j, c0=c0, gp=gp, sp2=sp2:
                                  e.activation(
                                      PT[gp][:, 512 * j + c0:512 * (j + 1)],
                                      psg[sp2][:, 512 * j + c0:512 * (j + 1)],
                                      AF.Exp),
                                  reads=[psg_B[sp2][j]], writes=[PT_B[gp]])
                else:
                    er = s.op("act", lambda e, gp=gp, sp2=sp2: e.activation(
                        PT[gp], psg[sp2], AF.Exp),
                        reads=[psg_B[sp2][0], psg_B[sp2][1]], writes=[PT_B[gp]])
                exp_prev, exp_last = exp_last, er
                for (kc, j, j0, c0) in cols:
                    if j0 >= 0:
                        s.op("dve", lambda e, j=j, c0=c0, gp=gp: e.tensor_mul(
                            PT[gp][:, 512 * j + c0:512 * j + c0 + P],
                            PT[gp][:, 512 * j + c0:512 * j + c0 + P], tril01),
                            reads=[PT_B[gp], tril_B], writes=[PT_B[gp]])
                pendq.append(dict(h=h, w=w, op=op_, cols=cols, gp=gp,
                                  last=(g + 2 >= nk)))
                if len(pendq) > 2:
                    emit_avs(pendq.pop(0))
                if g in (2, 4, 6):
                    if headq:
                        headq.pop(0)()
                    elif w1sq:
                        w1sq.pop(0)()
            if w == 1 and h + 1 < H:
                headq.extend(build_head_steps(h + 1))
            if w == NW - 1:
                while headq:
                    headq.pop(0)()
            if NW == 1 and h + 1 < H:
                build_head(h + 1)
    while w1sq:
        w1sq.pop(0)()
    for p in pendq:
        emit_avs(p)
    s.barrier(queues=("spq", "poolq"))
    if phases == "AB":
        s.final_wait("pool", "poolq")
        s.final_wait("sp", "spq")
        s.replay(nc)
        return nc

    Ofm_B = B_("Ofm"); den_B = B_("den")
    xwin_B = [B_("xwin0"), B_("xwin1")]
    xn2tm_B = B_("xn2tm"); xn2fm_B = [B_("xn2fm0"), B_("xn2fm1")]
    gring_B = [B_(f"g{i}") for i in range(16)]
    sw1t_B = [B_(f"sw1t{i}") for i in range(3)]
    sw2t_B = [B_(f"sw2t{i}") for i in range(2)]
    gtmp_B = [B_(f"gtmp{i}") for i in range(2)]
    sqC_B = [B_(f"sqC{i}") for i in range(2)]
    o2_B = [B_(f"o2_{t}") for t in range(4)]
    xmC_B = [B_(f"xmC{i}") for i in range(2)]
    outst_B = [B_(f"outst{i}") for i in range(2)]
    rows_B = B_("rows")
    abc_B = B_("abc")
    ystg_B = [B_(f"ystg{i}") for i in range(2)]
    bnst_B = [B_(f"bnst{i}") for i in range(2)]
    lncC_B = [B_(f"lncC{i}") for i in range(2)]
    lncD_B = [B_(f"lncD{i}") for i in range(2)]
    cps = [ps_big[:, 512 * i:512 * (i + 1)] for i in range(4)]
    cps_B = [B_(f"cps{i}") for i in range(4)]
    cps2 = ps_big[:, 1024:2048]
    cps2_B = B_("cps2w")

    def c_loads_od(w):
        s.op("pool", lambda e, w=w: e.dma_start(
            out=denom, in_=scr_o[w, :, 64, :]),
            reads=[scro_B[w][hh] for hh in range(H)], writes=[den_B],
            queue="poolq")
        for r in range(2):
            s.op("pool", lambda e, w=w, r=r: e.dma_start(
                out=O_fm[64 * r:64 * r + 64, :, :],
                in_=scr_o[w, r::2, 0:64, :].rearrange("g f q -> f g q")),
                reads=[scro_B[w][hh] for hh in range(r, H, 2)], writes=[Ofm_B],
                queue="poolq")

    def c_loads_x(w):
        wp = w % 2
        s.op("pool", lambda e, w=w, wp=wp: e.dma_start(
            out=x_win[wp],
            in_=x_in[4 * w:4 * w + 4].rearrange("t p d -> p t d")),
            writes=[xwin_B[wp]], queue="poolq")

    def c_loads(w):
        c_loads_od(w)
        c_loads_x(w)

    def c_stageA1(w):
        def _recs(e):
            with nc.allow_low_precision(reason="fp32r PE operand rounding"):
                return e.reciprocal(recS, denom)
        s.op("dve", _recs, reads=[den_B], writes=[recS_B])
        for c_ in range(8):
            pb, pbB = (ps_a, psA_B) if c_ % 2 == 0 else (ps_b, psB_B)
            s.op("pe", lambda e, c_=c_, pb=pb: e.matmul(
                pb, sel_r[:, P * c_:P * (c_ + 1)], recS,
                start=True, stop=True),
                reads=[selr_B, recS_B], writes=[pbB])
            s.op("dve", lambda e, c_=c_, pb=pb: e.tensor_mul(
                O_fm[:, c_, :], O_fm[:, c_, :], pb),
                reads=[Ofm_B, pbB], writes=[Ofm_B])

    def c_stageA2(w):
        for c_ in range(8):
            s.op("pe", lambda e, c_=c_: e.matmul(
                ps_a[0:1, :], ones128, O_fm[:, c_, :],
                start=(c_ == 0), stop=(c_ == 7)),
                reads=[ones_B, Ofm_B], writes=[psA_B], acc=True)
        s.op("dve", lambda e: e.tensor_scalar_mul(murow, ps_a[0:1, :], 1.0 / D),
             reads=[psA_B], writes=[rows_B])
        for c_ in range(8):
            sp_ = c_ % 2
            def _sq(e, c_=c_, sp_=sp_):
                with nc.allow_low_precision(
                        reason="fp32r PE operand rounding"):
                    return e.tensor_mul(
                        sq_sC[sp_], O_fm[:, c_, :], O_fm[:, c_, :])
            s.op("dve", _sq, reads=[Ofm_B], writes=[sqC_B[sp_]])
            s.op("pe", lambda e, c_=c_, sp_=sp_: e.matmul(
                ps_b[0:1, :], ones128r, sq_sC[sp_],
                start=(c_ == 0), stop=(c_ == 7)),
                reads=[onesr_B, sqC_B[sp_]], writes=[psB_B], acc=True)
        s.op("dve", lambda e: e.tensor_scalar_mul(e2row, ps_b[0:1, :], 1.0 / D),
             reads=[psB_B], writes=[rows_B])
        s.op("dve", lambda e: e.tensor_mul(tmprow, murow, murow),
             reads=[rows_B], writes=[rows_B])
        s.op("dve", lambda e: e.tensor_sub(varrow, e2row, tmprow),
             reads=[rows_B], writes=[rows_B])
        s.op("act", lambda e: e.activation(tmprow, varrow, AF.Sqrt,
                                           bias=consts[0:1, 0:1]),
             reads=[rows_B, consts_B], writes=[rows_B])
        s.op("dve", lambda e: e.reciprocal(arow, tmprow),
             reads=[rows_B], writes=[rows_B])
        s.op("dve", lambda e: e.scalar_tensor_tensor(
            out=nbrow, in0=murow, scalar=-1.0, in1=arow,
            op0=ALU.mult, op1=ALU.mult),
            reads=[rows_B], writes=[rows_B])
    def c_stageB1(w):
        wp = w % 2
        for t in range(4):
            tp = t % 2
            s.op("pe", lambda e, t=t, tp=tp: e.transpose(
                PSAB[tp][:, 0:1], arow[:, P * t:P * (t + 1)],
                ident_f[0:1, 0:1]),
                reads=[rows_B, identf_B], writes=[PSAB_B[tp]])
            s.op("pe", lambda e, t=t, tp=tp: e.transpose(
                PSAB[tp][:, 1:2], nbrow[:, P * t:P * (t + 1)],
                ident_f[0:1, 0:1]),
                reads=[rows_B, identf_B], writes=[PSAB_B[tp]])
            s.op("dve", lambda e, t=t, tp=tp: e.tensor_copy(
                abcol[:, 2 * t:2 * t + 2], PSAB[tp][:, 0:2]),
                reads=[PSAB_B[tp]], writes=[abc_B])
        for t in range(4):
            for cb in range(2):
                tp = (t * 2 + cb) % 2
                yp = (t * 2 + cb) % 2
                for c_ in range(4 * cb, 4 * cb + 4):
                    s.op("pe", lambda e, c_=c_, t=t, tp=tp, cb=cb: e.transpose(
                        PSAB[tp][:, P * (c_ - 4 * cb):P * (c_ - 4 * cb + 1)],
                        O_fm[:, c_, P * t:P * (t + 1)], ident_f),
                        reads=[Ofm_B, identf_B], writes=[PSAB_B[tp]],
                        acc=True)
                s.op("dve", lambda e, t=t, tp=tp, yp=yp: e.tensor_scalar(
                    out=ystg[yp], in0=PSAB[tp],
                    scalar1=abcol[:, 2 * t:2 * t + 1],
                    scalar2=abcol[:, 2 * t + 1:2 * t + 2],
                    op0=ALU.mult, op1=ALU.add),
                    reads=[PSAB_B[tp], abc_B], writes=[ystg_B[yp]])
                s.op("pool", lambda e, cb=cb, t=t, yp=yp, wp=wp: e.tensor_add(
                    x_win[wp][:, t, 512 * cb:512 * (cb + 1)],
                    x_win[wp][:, t, 512 * cb:512 * (cb + 1)], ystg[yp]),
                    reads=[ystg_B[yp], xwin_B[wp]], writes=[xwin_B[wp]])
        for t in range(4):
            par, b = t % 2, 8 * (t // 2)
            lc, lcB = lncolC[par], lncC_B[par]
            bs, bsB = bnst[par], bnst_B[par]
            for hf in range(2):
                s.op("dve", lambda e, t=t, bs=bs, hf=hf, wp=wp: e.bn_stats(
                    bs[:, 6 * hf:6 * hf + 6],
                    x_win[wp][:, t, 512 * hf:512 * (hf + 1)]),
                    reads=[xwin_B[wp]], writes=[bsB])
            s.op("dve", lambda e, lc=lc, b=b, bs=bs: e.bn_aggr(
                lc[:, b + 2:b + 4], bs),
                reads=[bsB], writes=[lcB])
        for t in range(4):
            par, b = t % 2, 8 * (t // 2)
            lc, lcB = lncolC[par], lncC_B[par]
            s.op("act", lambda e, lc=lc, b=b: e.activation(
                lc[:, b + 5:b + 6], lc[:, b + 3:b + 4], AF.Sqrt,
                bias=consts[:, 0:1]),
                reads=[lcB, consts_B], writes=[lcB])
        for t in range(4):
            par, b = t % 2, 8 * (t // 2)
            lc, lcB = lncolC[par], lncC_B[par]
            s.op("dve", lambda e, lc=lc, b=b: e.reciprocal(
                lc[:, b + 6:b + 7], lc[:, b + 5:b + 6]),
                reads=[lcB], writes=[lcB])
            s.op("dve", lambda e, lc=lc, b=b: e.scalar_tensor_tensor(
                out=lc[:, b + 7:b + 8], in0=lc[:, b + 2:b + 3],
                scalar=-1.0, in1=lc[:, b + 6:b + 7],
                op0=ALU.mult, op1=ALU.mult),
                reads=[lcB], writes=[lcB])
        for t in range(4):
            par, b = t % 2, 8 * (t // 2)
            lc, lcB = lncolC[par], lncC_B[par]
            s.op("dve", lambda e, t=t, lc=lc, b=b, wp=wp: e.tensor_scalar(
                out=xn2_tm[:, t, :], in0=x_win[wp][:, t, :],
                scalar1=lc[:, b + 6:b + 7], scalar2=lc[:, b + 7:b + 8],
                op0=ALU.mult, op1=ALU.add),
                reads=[xwin_B[wp], lcB], writes=[xn2tm_B])
    def c_stageB2(w):
        wp = w % 2
        for c0 in range(0, 8, 2):
            tb = (c0 // 2) % 2
            for c_ in (c0, c0 + 1):
                for t in range(4):
                    sl = (c_ - c0) * 4 + t
                    s.op("pe", lambda e, c_=c_, t=t, sl=sl, tb=tb: e.transpose(
                        TBF[tb][:, P * sl:P * (sl + 1)],
                        xn2_tm[:, t, P * c_:P * (c_ + 1)], ident_bf),
                        reads=[xn2tm_B, identbf_B], writes=[TBF_B[tb]])
            s.op("dve", lambda e, c0=c0, tb=tb, wp=wp: e.tensor_copy(
                xn2_fm[wp][:, c0:c0 + 2, :].rearrange("p c q -> p (c q)"),
                TBF[tb]),
                reads=[TBF_B[tb]], writes=[xn2fm_B[wp]])

    def c_epi_part1(w, t):
        par, b = t % 2, 8 * (t // 2)
        lc, lcB = lncolD[par], lncD_B[par]
        bs, bsB = bnst[par], bnst_B[par]
        for hf in range(2):
            s.op("dve", lambda e, t=t, bs=bs, hf=hf: e.bn_stats(
                bs[:, 6 * hf:6 * hf + 6],
                o2_sb[:, t, 512 * hf:512 * (hf + 1)]),
                reads=[o2_B[t]], writes=[bsB])
        s.op("dve", lambda e, lc=lc, b=b, bs=bs: e.bn_aggr(
            lc[:, b + 2:b + 4], bs),
            reads=[bsB], writes=[lcB])

    def c_epi_tail(w):
        wp = w % 2
        for t in range(4):
            par, b = t % 2, 8 * (t // 2)
            lc, lcB = lncolD[par], lncD_B[par]
            s.op("act", lambda e, lc=lc, b=b: e.activation(
                lc[:, b + 5:b + 6], lc[:, b + 3:b + 4], AF.Sqrt,
                bias=consts[:, 0:1]),
                reads=[lcB, consts_B], writes=[lcB])
        for t in range(4):
            par, b = t % 2, 8 * (t // 2)
            lc, lcB = lncolD[par], lncD_B[par]
            s.op("dve", lambda e, lc=lc, b=b: e.reciprocal(
                lc[:, b + 6:b + 7], lc[:, b + 5:b + 6]),
                reads=[lcB], writes=[lcB])
            s.op("dve", lambda e, lc=lc, b=b: e.scalar_tensor_tensor(
                out=lc[:, b + 7:b + 8], in0=lc[:, b + 2:b + 3],
                scalar=-1.0, in1=lc[:, b + 6:b + 7],
                op0=ALU.mult, op1=ALU.mult),
                reads=[lcB], writes=[lcB])
        for t in range(4):
            par, b = t % 2, 8 * (t // 2)
            lc, lcB = lncolD[par], lncD_B[par]
            s.op("dve", lambda e, t=t, par=par, lc=lc, b=b: e.tensor_scalar(
                out=o2_sb[:, t, :], in0=o2_sb[:, t, :],
                scalar1=lc[:, b + 6:b + 7], scalar2=lc[:, b + 7:b + 8],
                op0=ALU.mult, op1=ALU.add),
                reads=[o2_B[t], lcB], writes=[o2_B[t]])
            s.op("dve", lambda e, t=t, par=par, wp=wp: e.tensor_add(
                outstage[par], x_win[wp][:, t, :], o2_sb[:, t, :]),
                reads=[xwin_B[wp], o2_B[t]], writes=[outst_B[par]])
            s.op("pool", lambda e, w=w, t=t, par=par: e.dma_start(
                out=out_t[4 * w + t], in_=outstage[par]),
                reads=[outst_B[par]], queue="poolq")

    c_loads(0)
    c_stageA1(0)
    c_stageA2(0)
    c_stageB1(0)
    c_stageB2(0)
    if NW > 1:
        c_loads_od(1)
        c_stageA1(1)
    sw1_next = [0]

    def emit_sw1_load():
        gi = sw1_next[0]
        if gi >= 8 * NW:
            return
        sw1_next[0] += 1
        q = gi % 8
        s.op("sp", lambda e, q=q, gi=gi: e.dma_start(
            out=sw1t_sb[gi % 2],
            in_=scr_sw1t[4 * q:4 * q + 4].rearrange("h p c q -> p h c q")),
            reads=[scr1_B[hc] for hc in range(4 * q, 4 * q + 4)],
            writes=[sw1t_B[gi % 2]], queue="spq")

    emit_sw1_load()
    emit_sw1_load()
    for w in range(NW):
        wp = w % 2
        for grp in range(5):
            gb = grp % 2
            for k in (range(8) if grp < 4 else []):
                hc = 8 * grp + k
                s.op("sp", lambda e, hc=hc, k=k, gb=gb: e.dma_start(
                    out=sw2t_sb[gb][:, k, :].rearrange(
                        "p (d q) -> p d q", q=P),
                    in_=scr_sw2t[:, hc // 16, hc % 16].rearrange(
                        "d p q -> p d q")),
                    reads=[scr2h_B[dc][hc // 16] for dc in range(8)],
                    writes=[sw2t_B[gb]], queue="spq")
            for k in (range(8) if grp < 4 else []):
                hc = 8 * grp + k
                par = (hc // 4) % 2
                slot = hc % 16
                pq = hc % 2
                for c_ in range(8):
                    s.op("pe", lambda e, hc=hc, par=par, c_=c_, pq=pq, wp=wp:
                         e.matmul(
                             cps[pq], sw1t_sb[par][:, hc % 4, c_, :],
                             xn2_fm[wp][:, c_, :],
                             start=(c_ == 0), stop=(c_ == 7)),
                         reads=[sw1t_B[par], xn2fm_B[wp]], writes=[cps_B[pq]],
                         acc=True)
                if k % 4 == 3:
                    emit_sw1_load()
                s.op("act", lambda e, hc=hc, slot=slot, pq=pq: e.activation(
                    g_ring[:, slot, :], cps[pq], AF.Gelu,
                    bias=b1_sb[:, hc:hc + 1], scale=alpha1_sb[:, hc:hc + 1]),
                    reads=[cps_B[pq], b1s_B, a1_B], writes=[gring_B[slot]])
            fgrp = grp - 1
            fgb = fgrp % 2
            for t in (range(4) if fgrp >= 0 else []):
                for dh in range(2):
                    f2 = 2 + (t * 2 + dh) % 2
                    for k in range(8):
                        hc = 8 * fgrp + k
                        slot = hc % 16
                        s.op("pe", lambda e, slot=slot, t=t, dh=dh, k=k,
                             fgb=fgb, f2=f2: e.matmul(
                                 cps[f2], g_ring[:, slot, P * t:P * (t + 1)],
                                 sw2t_sb[fgb][:, k, 512 * dh:512 * (dh + 1)],
                                 start=(k == 0), stop=(k == 7)),
                             reads=[gring_B[slot], sw2t_B[fgb]],
                             writes=[cps_B[f2]], acc=True)
                    if fgrp == 0:
                        s.op("dve", lambda e, t=t, dh=dh, f2=f2: e.tensor_add(
                            o2_sb[:, t, 512 * dh:512 * (dh + 1)], cps[f2],
                            b2b[:, 512 * dh:512 * (dh + 1)]),
                            reads=[cps_B[f2], b2b_B], writes=[o2_B[t]])
                    else:
                        s.op("dve", lambda e, t=t, dh=dh, f2=f2: e.tensor_add(
                            o2_sb[:, t, 512 * dh:512 * (dh + 1)],
                            o2_sb[:, t, 512 * dh:512 * (dh + 1)], cps[f2]),
                            reads=[cps_B[f2], o2_B[t]], writes=[o2_B[t]])
                    if fgrp == 3 and dh == 1:
                        c_epi_part1(w, t)
            if grp == 0 and w + 1 < NW:
                c_stageA2(w + 1)
            elif grp == 1 and w + 1 < NW:
                c_loads_x(w + 1)
                c_stageB1(w + 1)
            elif grp == 2:
                if w + 1 < NW:
                    c_stageB2(w + 1)
                if w + 2 < NW:
                    c_loads_od(w + 2)
            elif grp == 3 and w + 2 < NW:
                c_stageA1(w + 2)
        c_epi_tail(w)
    nc._dbg = dict(O_fm=O_fm, denom=denom, x_win=x_win, xn2_tm=xn2_tm,
                   o2_sb=o2_sb, rowt=rowt, M_bf=M_bf, xn_tm=xn_tm)
    s.final_wait("pool", "poolq")
    s.final_wait("sp", "spq")
    s.replay(nc)
    return nc


def kernel(L0_attn_x, L0_ffn_w1, L0_ffn_b1, L0_ffn_w2, L0_ffn_b2):
    S = 4096
    x = np.ascontiguousarray(np.asarray(L0_attn_x, dtype=np.float32))
    w1 = np.ascontiguousarray(np.asarray(L0_ffn_w1, dtype=np.float32)).reshape(32, P, D)
    b1 = np.ascontiguousarray(np.asarray(L0_ffn_b1, dtype=np.float32)).reshape(32, P, 1)
    w2 = np.ascontiguousarray(np.asarray(L0_ffn_w2, dtype=np.float32)).reshape(8, P, HFF)
    b2 = np.ascontiguousarray(np.asarray(L0_ffn_b2, dtype=np.float32)).reshape(1, D)
    nc = build_program(S)
    in_maps = [{"x_in": np.ascontiguousarray(x[b].reshape(S // P, P, D)),
                "w1_in": w1, "b1_in": b1, "w2_in": w2, "b2_in": b2}
               for b in range(8)]
    res = run_bass_kernel_spmd(nc, in_maps, core_ids=list(range(8)))
    return np.stack([np.asarray(r["out"]).reshape(S, D) for r in res.results],
                    axis=0)
